# version 10
# baseline (speedup 1.0000x reference)
import numpy as np
import concourse.bass as bass
import concourse.mybir as mybir
from concourse.bass_utils import run_bass_kernel_spmd
from contextlib import ExitStack

F32 = mybir.dt.float32
BF16 = mybir.dt.bfloat16
I32 = mybir.dt.int32
AF = mybir.ActivationFunctionType
ALU = mybir.AluOpType

D = 4096
TP = 2048
NST = 16
RK = 128
INTER = 11008
GROUP = 128
NDT = D // 128
NNT = INTER // 128
TH = 1024
EPS = 1e-5

ENGINES = ("sync", "scalar", "vector", "tensor", "gpsimd")
DMA_SEM = {"sync": "dS", "scalar": "dA", "gpsimd": "dG"}
ENG_SEM = {"vector": "v", "scalar": "s", "tensor": "t", "gpsimd": "g", "sync": "q"}


class Plan:
    def __init__(self):
        self.items = {e: [] for e in ENGINES}
        self.val = dict(v=0, s=0, t=0, g=0, q=0)

    SERIAL = ("vector", "scalar", "gpsimd")

    def emit(self, eng, fn, sem=None, amt=1):
        if eng in self.SERIAL:
            own = ENG_SEM[eng]
            if self.val[own] > 0:
                self.items[eng].append(("w", None, own, self.val[own], 0))
            sem = own if sem is None else sem
        if sem is not None:
            self.val[sem] += amt
            self.items[eng].append(("e", fn, sem, self.val[sem], amt))
            if eng == "tensor":
                self.items[eng].append(("w", None, sem, self.val[sem], 0))
            return (sem, self.val[sem])
        self.items[eng].append(("e", fn, None, 0, 0))
        return None

    def step(self, eng, fn):
        return self.emit(eng, fn, sem=ENG_SEM[eng])

    def dma(self, queue, fn, ch=None):
        sem = "d_" + (ch if ch is not None else DMA_SEM[queue])
        self.val[sem] = self.val.get(sem, 0) + 16
        self.items[queue].append(("e", fn, sem, self.val[sem], 16))
        return (sem, self.val[sem])

    def wait(self, eng, token):
        if token is None:
            return
        self.items[eng].append(("w", None, token[0], token[1], 0))

    def waits(self, eng, tokens):
        best = {}
        for t in tokens:
            if t is None:
                continue
            s, v = t
            if best.get(s, -1) < v:
                best[s] = v
        for s, v in best.items():
            self.items[eng].append(("w", None, s, v, 0))

    def cur(self, sem):
        return (sem, self.val[sem])

    def barrier(self):
        dma_toks = [self.cur(s) for s in list(self.val) if s.startswith("d_")]
        toks = {}
        for eng in ENGINES:
            self.waits(eng, dma_toks)
            toks[eng] = self.emit(eng, lambda h: h.nop(), sem=ENG_SEM[eng])
        for eng in ENGINES:
            self.waits(eng, [toks[o] for o in ENGINES if o != eng])

    def replay(self, eng_name, handle, sems):
        for it in self.items[eng_name]:
            if it[0] == "w":
                handle.wait_ge(sems[it[2]], it[3])
            else:
                ins = it[1](handle)
                if it[2] is not None:
                    ins.then_inc(sems[it[2]], it[4])


class Buf:

    def __init__(self, plan):
        self.P = plan
        self.w = None
        self.readers = []

    def write(self, eng, fn, sem=None, dma=False, ch=None):
        self.P.waits(eng, self.readers + [self.w])
        if dma:
            tok = self.P.dma(eng, fn, ch=ch)
        elif sem is False:
            tok = self.P.emit(eng, fn)
        else:
            tok = self.P.step(eng, fn)
        self.w = tok
        self.readers = []
        return tok

    def read(self, eng, fn, sem=None, dma=False, ch=None):
        self.P.wait(eng, self.w)
        if dma:
            tok = self.P.dma(eng, fn, ch=ch)
        elif sem is False:
            tok = self.P.emit(eng, fn)
            tok = self.P.cur(ENG_SEM[eng]) if eng != "sync" else self.P.cur("q")
        else:
            tok = self.P.step(eng, fn)
        self.readers.append(tok)
        return tok

    def rw(self, eng, fn):
        self.P.waits(eng, self.readers + [self.w])
        tok = self.P.step(eng, fn)
        self.w = tok
        self.readers = []
        return tok

    def note_read(self, tok):
        self.readers.append(tok)

    def note_write(self, tok):
        self.w = tok
        self.readers = []


def build_program():
    nc = bass.Bass("TRN2", target_bir_lowering=False)
    Tx = {}

    def din(name, shape, dtype):
        Tx[name] = nc.dram_tensor(name, shape, dtype, kind="ExternalInput").ap()

    din("x1T", [D, TP], F32)
    din("x2T", [D, TP], F32)
    din("carry", [128, 1], F32)
    din("xpwT", [D, 160], F32)
    din("dtwT", [RK, D], F32)
    din("dtb", [128, NDT], F32)
    din("alog", [128, NDT * NST], F32)
    din("dvec", [128, NDT], F32)
    din("shif8", [128, 1], I32)
    din("shifr", [128, 86], I32)
    din("gqT", [INTER, D // 8], I32)
    din("gsT", [INTER, D // GROUP], F32)
    din("gzR", [INTER, D // GROUP], I32)
    din("uqT", [INTER, D // 8], I32)
    din("usT", [INTER, D // GROUP], F32)
    din("uzR", [INTER, D // GROUP], I32)
    din("dqT", [D, INTER // 8], I32)
    din("dsT", [D, INTER // GROUP], F32)
    din("dzR", [D, INTER // GROUP], I32)
    Tx["outT"] = nc.dram_tensor("outT", [D, TP], F32, kind="ExternalOutput").ap()

    def dint(name, shape, dtype):
        Tx[name] = nc.dram_tensor(name, shape, dtype).ap()

    dint("wgt_dram", [INTER, D], BF16)
    dint("wut_dram", [INTER, D], BF16)
    dint("wdt_dram", [D, INTER], BF16)
    dint("wd_dram", [INTER, D], BF16)
    dint("dt_dram", [D, TP], F32)
    dint("dtx_dram", [D, TP], F32)
    dint("bc_dram", [32, TP], F32)
    dint("y_dram", [D, TP], F32)
    dint("xmid_dram", [D, TP], F32)
    dint("h_dram", [INTER, TP], BF16)

    P = Plan()

    with ExitStack() as ctx:
        E = ctx.enter_context
        sb = {}

        def S(stack, name, shape, dtype):
            sb[name] = stack.enter_context(nc.sbuf_tensor(name, shape, dtype))
            return sb[name]

        S(ctx, "ones_bf", [128, 1], BF16)
        S(ctx, "ones_f", [1, 128], F32)
        S(ctx, "carry_c", [128, 1], F32)
        S(ctx, "shif8_c", [128, 1], I32)
        S(ctx, "shifr_c", [128, 86], I32)
        S(ctx, "srow", [1, TP], F32)
        S(ctx, "srow2", [1, TP], F32)
        S(ctx, "srow3", [1, TP], F32)
        S(ctx, "m_rep", [128, TP], F32)
        S(ctx, "r_rep", [128, TP], F32)
        S(ctx, "hstate", [128, NDT * NST], F32)
        S(ctx, "minit", [128, NDT * NST], F32)

        S(ctx, "cvals", [1, 2], F32)
        P.emit("vector", lambda h: h.memset(sb["cvals"][0:1, 0:1], EPS))
        P.step("vector", lambda h: h.memset(sb["cvals"][0:1, 1:2], -1.0))
        nc.const_aps.register(F32, EPS, sb["cvals"][0:1, 0:1])
        nc.const_aps.register(F32, -1.0, sb["cvals"][0:1, 1:2])

        P.dma("sync", lambda h: h.dma_start(out=sb["carry_c"][:], in_=Tx["carry"]), ch="cst")
        P.dma("sync", lambda h: h.dma_start(out=sb["shif8_c"][:], in_=Tx["shif8"]), ch="cst")
        tok_c0 = P.dma("sync", lambda h: h.dma_start(out=sb["shifr_c"][:], in_=Tx["shifr"]), ch="cst")
        P.waits("vector", [tok_c0])
        P.step("vector", lambda h: h.memset(sb["ones_bf"][:], 1.0))
        P.step("vector", lambda h: h.memset(sb["ones_f"][:], 1.0))
        P.step("vector", lambda h: h.memset(sb["hstate"][:], 0.0))

        with ExitStack() as st:
            S(st, "qt0", [128, INTER // 8], I32)
            S(st, "qt1", [128, INTER // 8], I32)
            S(st, "w4", [128, INTER], I32)
            S(st, "wbf0", [128, INTER], BF16)
            S(st, "wbf1", [128, INTER], BF16)
            S(st, "zq", [128, 86], I32)
            S(st, "z1", [128, 86], F32)
            S(st, "zs", [128, 86], F32)
            S(st, "scg", [128, 86], F32)
            phase_dequant(nc, P, sb, Tx)
            P.barrier()

        with ExitStack() as st:
            S(st, "identi", [128, 128], I32)
            S(st, "identw", [128, 128], BF16)
            S(st, "wtin0", [128, 512], BF16)
            S(st, "wtin1", [128, 512], BF16)
            S(st, "wstage4", [128, 4 * D], BF16)
            with ExitStack() as stp:
                ptr = [stp.enter_context(nc.psum_tensor(f"ptr{i}", [128, 512], BF16))
                       for i in range(8)]
                phase_transpose(nc, P, sb, Tx, ptr)
            P.barrier()

        with ExitStack() as st:
            S(st, "a_c", [128, NDT * NST], F32)
            S(st, "dvec_c", [128, NDT], F32)
            sb["bcl"] = sb["srow3"]

            with ExitStack() as sta:
                S(sta, "alog_c", [128, NDT * NST], F32)
                P.dma("sync", lambda h: h.dma_start(out=sb["alog_c"][:], in_=Tx["alog"]), ch="cst")
                tok_cl = P.dma("sync", lambda h: h.dma_start(out=sb["dvec_c"][:], in_=Tx["dvec"]), ch="cst")
                P.waits("scalar", [tok_cl])
                tok_ae = P.step("scalar", lambda h: h.activation(
                    sb["a_c"][:], sb["alog_c"][:], AF.Exp))
                P.wait("vector", tok_ae)
                P.step("vector", lambda h: h.tensor_scalar(
                    sb["a_c"][:], sb["a_c"][:], -1.0, None, ALU.mult))
                P.barrier()

            for p in (0, 1):
                xsrc = Tx["x1T"] if p == 0 else Tx["x2T"]
                with ExitStack() as st0:
                    S(st0, f"xpw_bf{p}", [128, NDT * 160], BF16)
                    S(st0, f"dtw_bf{p}", [128, D], BF16)
                    S(st0, f"dtb_c{p}", [128, NDT], F32)
                    S(st0, f"xt0_{p}", [128, TP], F32)
                    S(st0, f"xt1_{p}", [128, TP], F32)
                    S(st0, f"xb0_{p}", [128, TP], BF16)
                    S(st0, f"xsq_{p}", [128, TP], BF16)
                    S(st0, f"xln_{p}", [128, TP], F32)
                    S(st0, f"tA_{p}", [128, TP], F32)
                    S(st0, f"tB_{p}", [128, TP], F32)
                    S(st0, f"tC_{p}", [128, TP], F32)
                    S(st0, f"dtin_bf{p}", [128, TP], BF16)
                    S(st0, f"xp_acc{p}", [128, TP], F32)
                    S(st0, f"bc_acc{p}", [32, TP], F32)
                    tok_dtb = P.dma("sync", (lambda p_: lambda h: h.dma_start(
                        out=sb[f"dtb_c{p_}"][:], in_=Tx["dtb"]))(p), ch="cst")
                    tok_xpw = P.dma("gpsimd", (lambda p_: lambda h: h.dma_start(
                        out=sb[f"xpw_bf{p_}"][:].rearrange("p (n r) -> p n r", r=160),
                        in_=Tx["xpwT"].rearrange("(n p) r -> p n r", p=128)))(p), ch="gcst")
                    tok_dtw = P.dma("gpsimd", (lambda p_: lambda h: h.dma_start(
                        out=sb[f"dtw_bf{p_}"][:], in_=Tx["dtwT"]))(p), ch="gcst")
                    P.waits("tensor", [tok_xpw, tok_dtw, tok_dtb])
                    P.waits("scalar", [tok_dtb])
                    with nc.psum_tensor(f"pxa{p}", [128, TP], F32) as pxa, \
                         nc.psum_tensor(f"pxb{p}", [32, TP], F32) as pxb:
                        phase_s0(nc, P, sb, Tx, pxa, pxb, p, xsrc)
                    P.barrier()
                with ExitStack() as st1:
                    for nm in ("brep0", "brep1", "brep2", "brep3",
                               "crep0", "crep1", "crep2", "crep3",
                               "sdt0", "sdt1", "sdx0", "sdx1",
                               "da0", "da1", "uu0", "uu1", "hh0", "hh1",
                               "yacc"):
                        S(st1, f"{nm}_{p}", [128, TP], F32)
                    with nc.psum_tensor(f"prp{p}", [128, TP], F32) as prp:
                        phase_s1(nc, P, sb, Tx, prp, p)
                    P.barrier()
            with ExitStack() as st2:
                S(st2, "xt0_2", [128, TP], F32)
                S(st2, "xt1_2", [128, TP], F32)
                S(st2, "yld0", [128, TP], F32)
                S(st2, "yld1", [128, TP], F32)
                S(st2, "xln_2", [128, TP], F32)
                phase_s2(nc, P, sb, Tx)
                P.barrier()

        with nc.psum_tensor("pfa", [128, TP], F32) as pfa, \
             nc.psum_tensor("pfb", [128, TP], F32) as pfb:
            with ExitStack() as st:
                S(st, "fb0", [128, TP], BF16)
                S(st, "fsq", [128, TP], BF16)
                S(st, "xm0", [128, TP], F32)
                S(st, "xm1", [128, TP], F32)
                ffn_ln2(nc, P, sb, Tx, pfa, pfb)
                P.barrier()
            for th in range(2):
                with ExitStack() as stg:
                    S(stg, f"fln{th}", [128, TH], F32)
                    S(stg, f"fxm0_{th}", [128, TH], F32)
                    S(stg, f"fxm1_{th}", [128, TH], F32)
                    S(stg, f"xln2h{th}", [128, NDT * TH], BF16)
                    S(stg, f"wcA0_{th}", [128, 32 * 256], BF16)
                    S(stg, f"wcA1_{th}", [128, 32 * 256], BF16)
                    S(stg, f"wcB0_{th}", [128, 32 * 256], BF16)
                    S(stg, f"wcB1_{th}", [128, 32 * 256], BF16)
                    S(stg, f"gsil{th}", [128, 512], F32)
                    S(stg, f"hbf{th}", [128, TH], BF16)
                    ffn_gateup(nc, P, sb, Tx, pfa, pfb, th)
                    P.barrier()
                with ExitStack() as std:
                    S(std, f"wdL{th}", [128, 44 * 512], BF16)
                    S(std, f"wdH{th}", [128, 42 * 512], BF16)
                    S(std, f"hg0_{th}", [128, 11 * TH], BF16)
                    S(std, f"hg1_{th}", [128, 11 * TH], BF16)
                    S(std, f"oacc0_{th}", [128, TH], F32)
                    S(std, f"oacc1_{th}", [128, TH], F32)
                    S(std, f"oacc2_{th}", [128, TH], F32)
                    S(std, f"oacc3_{th}", [128, TH], F32)
                    S(std, f"oout{th}", [128, TH], F32)
                    S(std, f"dxm0_{th}", [128, TH], F32)
                    S(std, f"dxm1_{th}", [128, TH], F32)
                    ffn_down(nc, P, sb, Tx, pfa, pfb, th)
                    P.barrier()

        P.waits("sync", [P.cur(s) for s in list(P.val) if s.startswith("d_")])

        sems = {n: E(nc.semaphore(n.replace("d_", "D"))) for n in P.val}
        block = E(nc.Block())

        @block.sync
        def _(h):
            P.replay("sync", h, sems)

        @block.scalar
        def _(h):
            P.replay("scalar", h, sems)

        @block.vector
        def _(h):
            P.replay("vector", h, sems)

        @block.tensor
        def _(h):
            P.replay("tensor", h, sems)

        @block.gpsimd
        def _(h):
            P.replay("gpsimd", h, sems)

    return nc


def phase_dequant(nc, P, sb, Tx):
    jobs = [
        ("gqT", "gsT", "gzR", "wgt_dram", INTER, D),
        ("uqT", "usT", "uzR", "wut_dram", INTER, D),
        ("dqT", "dsT", "dzR", "wdt_dram", D, INTER),
    ]
    q_b = [Buf(P), Buf(P)]
    w_b = [Buf(P), Buf(P)]
    z_b, s_b, z1_b, w4_b = Buf(P), Buf(P), Buf(P), Buf(P)
    for qn, sn, zn, wn, N, K in jobs:
        KP, G = K // 8, K // GROUP
        for nt in range(N // 128):
            b = nt % 2
            qb = "qt0" if b == 0 else "qt1"
            wb = "wbf0" if b == 0 else "wbf1"
            n0 = nt * 128
            q_b[b].write("sync", (lambda qb_, qn_, n0_, KP_: lambda h: h.dma_start(
                out=sb[qb_][:, 0:KP_], in_=Tx[qn_][n0_:n0_ + 128, :]))(qb, qn, n0, KP),
                dma=True, ch=f"q{b}")
            z_b.write("sync", (lambda zn_, n0_, G_: lambda h: h.dma_start(
                out=sb["zq"][:, 0:G_], in_=Tx[zn_][n0_:n0_ + 128, :]))(zn, n0, G),
                dma=True, ch="zs")
            s_b.write("sync", (lambda sn_, n0_, G_: lambda h: h.dma_start(
                out=sb["scg"][:, 0:G_], in_=Tx[sn_][n0_:n0_ + 128, :]))(sn, n0, G),
                dma=True, ch="zs")
            z_b.w = s_b.w
            z_b.rw("vector", (lambda G_: lambda h: h.tensor_tensor(
                sb["zq"][:, 0:G_], sb["zq"][:, 0:G_], sb["shifr_c"][:, 0:G_],
                ALU.logical_shift_right))(G))
            z_b.rw("vector", (lambda G_: lambda h: h.tensor_scalar(
                sb["zq"][:, 0:G_], sb["zq"][:, 0:G_], 15, None, ALU.bitwise_and))(G))
            tok = z1_b.write("vector", (lambda G_: lambda h: h.tensor_scalar(
                sb["z1"][:, 0:G_], sb["zq"][:, 0:G_], 1, None, ALU.add))(G))
            z_b.note_read(tok)
            P.wait("vector", s_b.w)
            tok_zs = z1_b.read("vector", (lambda G_: lambda h: h.scalar_tensor_tensor(
                sb["zs"][:, 0:G_], sb["z1"][:, 0:G_], -1.0, sb["scg"][:, 0:G_],
                ALU.mult, ALU.mult))(G))
            s_b.note_read(tok_zs)
            P.wait("vector", q_b[b].w)
            for j in range(8):
                tok = w4_b.rw("vector", (lambda j_, qb_, KP_, K_: lambda h: h.tensor_scalar(
                    sb["w4"][:, j_:K_:8], sb[qb_][:, 0:KP_], 4 * j_, 15,
                    ALU.logical_shift_right, ALU.bitwise_and))(j, qb, KP, K))
                q_b[b].note_read(tok)
            P.waits("vector", w_b[b].readers + [w_b[b].w])
            P.waits("scalar", w_b[b].readers + [w_b[b].w])
            P.waits("scalar", [w4_b.w, tok_zs])
            last_v = last_s = None
            for g in range(G):
                if g % 3 == 2:
                    fn = (lambda g_, wb_: lambda h: h.activation(
                        sb[wb_][:, g_ * GROUP:(g_ + 1) * GROUP],
                        sb["w4"][:, g_ * GROUP:(g_ + 1) * GROUP],
                        AF.Identity, scale=sb["scg"][:, g_:g_ + 1],
                        bias=sb["zs"][:, g_:g_ + 1]))(g, wb)
                    last_s = P.step("scalar", fn)
                else:
                    fn = (lambda g_, wb_: lambda h: h.tensor_scalar(
                        sb[wb_][:, g_ * GROUP:(g_ + 1) * GROUP],
                        sb["w4"][:, g_ * GROUP:(g_ + 1) * GROUP],
                        sb["z1"][:, g_:g_ + 1], sb["scg"][:, g_:g_ + 1],
                        ALU.subtract, ALU.mult))(g, wb)
                    last_v = P.step("vector", fn)
            for t in (last_v, last_s):
                w4_b.note_read(t)
                z1_b.note_read(t)
                s_b.note_read(t)
            w_b[b].w = last_v
            w_b[b].readers = []
            P.wait("sync", last_s)
            w_b[b].read("sync", (lambda wb_, wn_, n0_, K_: lambda h: h.dma_start(
                out=Tx[wn_][n0_:n0_ + 128, :], in_=sb[wb_][:, 0:K_]))(wb, wn, n0, K),
                dma=True, ch=f"ws{b}")
            w_b[b].readers.append(last_s)


def phase_transpose(nc, P, sb, Tx, ptr):
    tok_io = P.step("gpsimd", lambda h: h.iota(
        sb["identi"][:], [[1, 128]], base=0, channel_multiplier=-1))
    P.wait("vector", tok_io)
    tok_id = P.step("vector", lambda h: h.tensor_scalar(
        sb["identw"][:], sb["identi"][:], 0, None, ALU.is_equal))
    P.wait("tensor", tok_id)

    N, K = D, INTER
    l_b = [Buf(P), Buf(P)]
    ps_b = [Buf(P), Buf(P)]
    stage = Buf(P)
    kblocks = [list(range(g, min(g + 4, K // 128))) for g in range(0, K // 128, 4)]
    for kb in kblocks:
        nk = len(kb)
        k0 = kb[0] * 128
        for nt in range(N // 128):
            b = nt % 2
            lb = "wtin0" if b == 0 else "wtin1"
            l_b[b].write("sync", (lambda lb_, nt_, k0_, nk_: lambda h: h.dma_start(
                out=sb[lb_][:, 0:nk_ * 128],
                in_=Tx["wdt_dram"][nt_ * 128:nt_ * 128 + 128, k0_:k0_ + nk_ * 128]))(
                lb, nt, k0, nk), dma=True, ch=f"tl{b}")
            P.wait("tensor", l_b[b].w)
            P.waits("tensor", ps_b[b].readers)
            tok_tr = None
            for j in range(nk):
                fn = (lambda lb_, j_, b_: lambda h: h.transpose(
                    ptr[4 * b_ + j_][:, 0:128], sb[lb_][:, j_ * 128:(j_ + 1) * 128],
                    sb["identw"][:]))(lb, j, b)
                tok_tr = P.step("tensor", fn) if j == nk - 1 else P.emit("tensor", fn)
            l_b[b].note_read(tok_tr)
            ps_b[b].note_write(tok_tr)
            P.wait("scalar", tok_tr)
            if nt < 2:
                P.wait("scalar", stage.readers[-1] if stage.readers else None)
            tok_ev = None
            for j in range(nk):
                fn = (lambda j_, nt_, b_: lambda h: h.copy(
                    sb["wstage4"][:, j_ * N + nt_ * 128:j_ * N + nt_ * 128 + 128],
                    ptr[4 * b_ + j_][:, 0:128]))(j, nt, b)
                tok_ev = P.step("scalar", fn) if j == nk - 1 else P.emit("scalar", fn)
            ps_b[b].note_read(tok_ev)
            stage.note_write(tok_ev) if nt == N // 128 - 1 else None
        stage.note_write(tok_ev)
        for j, kt in enumerate(kb):
            stage.read("sync", (lambda kt_, j_: lambda h: h.dma_start(
                out=Tx["wd_dram"][kt_ * 128:kt_ * 128 + 128, :],
                in_=sb["wstage4"][:, j_ * N:(j_ + 1) * N]))(kt, j), dma=True, ch="tst")


def _stats_sweep(nc, P, sb, Tx, pfa, pfb, src_dram, xbufs, bfn, sqn):
    xb_b = [Buf(P), Buf(P)]
    bf_b, sq_b = Buf(P), Buf(P)
    row_b = Buf(P)
    ps_b = Buf(P)
    P.step("vector", lambda h: h.memset(sb["srow"][0:1, :], 0.0))
    tok0 = P.step("vector", lambda h: h.memset(sb["srow2"][0:1, :], 0.0))
    row_b.note_write(tok0)
    for dt_i in range(NDT):
        b = dt_i % 2
        xb = xbufs[b]
        xb_b[b].write("sync", (lambda xb_, d_: lambda h: h.dma_start(
            out=sb[xb_][:], in_=src_dram[d_ * 128:d_ * 128 + 128, :]))(xb, dt_i),
            dma=True, ch=f"x{b}")
        P.wait("vector", xb_b[b].w)
        tok_cp = bf_b.write("vector", (lambda xb_: lambda h: h.tensor_copy(
            sb[bfn][:], sb[xb_][:]))(xb))
        xb_b[b].note_read(tok_cp)
        P.wait("scalar", xb_b[b].w)
        tok_sq = sq_b.write("scalar", (lambda xb_: lambda h: h.activation(
            sb[sqn][:], sb[xb_][:], AF.Square))(xb))
        xb_b[b].note_read(tok_sq)
        P.waits("tensor", [tok_cp, tok_sq] + ps_b.readers)
        for tc in range(4):
            P.emit("tensor", (lambda tc_: lambda h: h.matmul(
                pfa[0:1, tc_ * 512:(tc_ + 1) * 512], sb["ones_bf"][:, 0:1],
                sb[bfn][:, tc_ * 512:(tc_ + 1) * 512],
                start=True, stop=True))(tc))
        tok_mm = None
        for tc in range(4):
            fn = (lambda tc_: lambda h: h.matmul(
                pfb[0:1, tc_ * 512:(tc_ + 1) * 512], sb["ones_bf"][:, 0:1],
                sb[sqn][:, tc_ * 512:(tc_ + 1) * 512],
                start=True, stop=True))(tc)
            tok_mm = P.step("tensor", fn) if tc == 3 else P.emit("tensor", fn)
        bf_b.note_read(tok_mm)
        sq_b.note_read(tok_mm)
        ps_b.note_write(tok_mm)
        P.wait("vector", tok_mm)
        P.emit("vector", lambda h: h.tensor_tensor(
            sb["srow"][0:1, :], sb["srow"][0:1, :], pfa[0:1, 0:TP], ALU.add))
        tok_acc = P.step("vector", lambda h: h.tensor_tensor(
            sb["srow2"][0:1, :], sb["srow2"][0:1, :], pfb[0:1, 0:TP], ALU.add))
        ps_b.note_read(tok_acc)
        row_b.note_write(tok_acc)
    P.emit("vector", lambda h: h.tensor_scalar(
        sb["srow"][0:1, :], sb["srow"][0:1, :], 1.0 / D, None, ALU.mult))
    P.emit("vector", lambda h: h.tensor_scalar(
        sb["srow2"][0:1, :], sb["srow2"][0:1, :], 1.0 / D, None, ALU.mult))
    P.emit("vector", lambda h: h.tensor_tensor(
        sb["srow3"][0:1, :], sb["srow"][0:1, :], sb["srow"][0:1, :], ALU.mult))
    tok_var = P.step("vector", lambda h: h.tensor_tensor(
        sb["srow2"][0:1, :], sb["srow2"][0:1, :], sb["srow3"][0:1, :], ALU.subtract))
    P.wait("scalar", tok_var)
    tok_sqr = P.step("scalar", lambda h: h.activation(
        sb["srow2"][0:1, :], sb["srow2"][0:1, :], AF.Sqrt, bias=EPS))
    P.wait("vector", tok_sqr)
    tok_rc = P.step("vector", lambda h: h.reciprocal(
        sb["srow2"][0:1, :], sb["srow2"][0:1, :]))
    P.waits("tensor", [tok_rc] + ps_b.readers)
    tok_rep = None
    for tc in range(4):
        tok_rep = P.step("tensor", (lambda tc_: lambda h: h.matmul(
            pfa[:, tc_ * 512:(tc_ + 1) * 512], sb["ones_f"][0:1, :],
            sb["srow"][0:1, tc_ * 512:(tc_ + 1) * 512], start=True, stop=True))(tc))
    P.wait("vector", tok_rep)
    tok_me = P.step("vector", lambda h: h.tensor_copy(sb["m_rep"][:], pfa[:, 0:TP]))
    P.wait("tensor", tok_me)
    for tc in range(4):
        tok_rep = P.step("tensor", (lambda tc_: lambda h: h.matmul(
            pfa[:, tc_ * 512:(tc_ + 1) * 512], sb["ones_f"][0:1, :],
            sb["srow2"][0:1, tc_ * 512:(tc_ + 1) * 512], start=True, stop=True))(tc))
    P.wait("vector", tok_rep)
    P.step("vector", lambda h: h.tensor_copy(sb["r_rep"][:], pfa[:, 0:TP]))


def phase_s0(nc, P, sb, Tx, pxa, pxb, p, xsrc):
    xt0, xt1 = f"xt0_{p}", f"xt1_{p}"
    xb0, xsq, xln = f"xb0_{p}", f"xsq_{p}", f"xln_{p}"
    tA, tB, tC = f"tA_{p}", f"tB_{p}", f"tC_{p}"
    dtin_bf = f"dtin_bf{p}"
    xp_acc, bc_acc = f"xp_acc{p}", f"bc_acc{p}"
    xpw_bf, dtw_bf, dtb_c = f"xpw_bf{p}", f"dtw_bf{p}", f"dtb_c{p}"
    _stats_sweep(nc, P, sb, Tx, pxa, pxb, xsrc, (xt0, xt1), xb0, xsq)
    P.barrier()

    xb_b = [Buf(P), Buf(P)]
    ln_b, lbf_b = Buf(P), Buf(P)
    acc_b, ps_b = Buf(P), Buf(P)
    P.step("vector", lambda h: h.memset(sb[xp_acc][:], 0.0))
    tok0 = P.step("vector", lambda h: h.memset(sb[bc_acc][0:32, :], 0.0))
    acc_b.note_write(tok0)
    for dt_i in range(NDT):
        b = dt_i % 2
        xb = xt0 if b == 0 else xt1
        xb_b[b].write("sync", (lambda xb_, d_: lambda h: h.dma_start(
            out=sb[xb_][:], in_=xsrc[d_ * 128:d_ * 128 + 128, :]))(xb, dt_i),
            dma=True, ch=f"x{b}")
        P.wait("vector", xb_b[b].w)
        ln_b.write("vector", (lambda xb_: lambda h: h.tensor_tensor(
            sb[xln][:], sb[xb_][:], sb["m_rep"][:], ALU.subtract))(xb))
        tok_ln = ln_b.rw("vector", lambda h: h.tensor_tensor(
            sb[xln][:], sb[xln][:], sb["r_rep"][:], ALU.mult))
        xb_b[b].note_read(tok_ln)
        tok_bf = lbf_b.write("vector", lambda h: h.tensor_copy(sb[xb0][:], sb[xln][:]))
        ln_b.note_read(tok_bf)
        P.waits("tensor", [tok_bf] + ps_b.readers)
        for tc in range(4):
            P.emit("tensor", (lambda tc_, d_: lambda h: h.matmul(
                pxa[:, tc_ * 512:(tc_ + 1) * 512],
                sb[xpw_bf][:, d_ * 160:d_ * 160 + 128],
                sb[xb0][:, tc_ * 512:(tc_ + 1) * 512],
                start=True, stop=True))(tc, dt_i))
        tok_mm = None
        for tc in range(4):
            fn = (lambda tc_, d_: lambda h: h.matmul(
                pxb[0:32, tc_ * 512:(tc_ + 1) * 512],
                sb[xpw_bf][:, d_ * 160 + 128:d_ * 160 + 160],
                sb[xb0][:, tc_ * 512:(tc_ + 1) * 512],
                start=True, stop=True))(tc, dt_i)
            tok_mm = P.step("tensor", fn) if tc == 3 else P.emit("tensor", fn)
        lbf_b.note_read(tok_mm)
        ps_b.note_write(tok_mm)
        P.wait("vector", tok_mm)
        P.emit("vector", lambda h: h.tensor_tensor(
            sb[xp_acc][:], sb[xp_acc][:], pxa[:, 0:TP], ALU.add))
        tok_acc = P.step("vector", lambda h: h.tensor_tensor(
            sb[bc_acc][0:32, :], sb[bc_acc][0:32, :], pxb[0:32, 0:TP], ALU.add))
        ps_b.note_read(tok_acc)
        acc_b.note_write(tok_acc)
    tok_dtin = P.step("vector", (lambda: lambda h: h.tensor_copy(
        sb[dtin_bf][:], sb[xp_acc][:]))())
    P.wait("sync", tok_dtin)
    P.dma("sync", lambda h: h.dma_start(
        out=Tx["bc_dram"][:, :], in_=sb[bc_acc][0:32, :]), ch="bc")
    P.barrier()

    xb_b = [Buf(P), Buf(P)]
    tA_b, tB_b, tC_b, ln_b = Buf(P), Buf(P), Buf(P), Buf(P)
    ps_b = Buf(P)
    for dt_i in range(NDT):
        b = dt_i % 2
        xb = xt0 if b == 0 else xt1
        xb_b[b].write("sync", (lambda xb_, d_: lambda h: h.dma_start(
            out=sb[xb_][:], in_=xsrc[d_ * 128:d_ * 128 + 128, :]))(xb, dt_i),
            dma=True, ch=f"x{b}")
        tok_dmm = None
        P.waits("tensor", ps_b.readers)
        for tc in range(4):
            tok_dmm = P.step("tensor", (lambda tc_, d_: lambda h: h.matmul(
                pxa[:, tc_ * 512:(tc_ + 1) * 512],
                sb[dtw_bf][:, d_ * 128:d_ * 128 + 128],
                sb[dtin_bf][:, tc_ * 512:(tc_ + 1) * 512],
                start=True, stop=True))(tc, dt_i))
        ps_b.note_write(tok_dmm)
        P.wait("scalar", tok_dmm)
        t1 = tA_b.write("scalar", (lambda d_: lambda h: h.activation(
            sb[tA][:], pxa[:, 0:TP], AF.Relu, bias=sb[dtb_c][:, d_:d_ + 1]))(dt_i))
        t2 = tB_b.write("scalar", (lambda d_: lambda h: h.activation(
            sb[tB][:], pxa[:, 0:TP], AF.Abs, bias=sb[dtb_c][:, d_:d_ + 1]))(dt_i))
        ps_b.note_read(t1)
        ps_b.note_read(t2)
        tok3 = tC_b.write("scalar", lambda h: h.activation(
            sb[tC][:], sb[tB][:], AF.Exp, scale=-1.0))
        tB_b.note_read(tok3)
        tok4 = tB_b.write("scalar", lambda h: h.activation(
            sb[tB][:], sb[tC][:], AF.Ln, bias=1.0))
        tC_b.note_read(tok4)
        P.waits("vector", [tA_b.w, tB_b.w])
        tok_dt = tC_b.write("vector", lambda h: h.tensor_tensor(
            sb[tC][:], sb[tA][:], sb[tB][:], ALU.add))
        tA_b.note_read(tok_dt)
        tB_b.note_read(tok_dt)
        P.wait("vector", xb_b[b].w)
        ln_b.write("vector", (lambda xb_: lambda h: h.tensor_tensor(
            sb[xln][:], sb[xb_][:], sb["m_rep"][:], ALU.subtract))(xb))
        tok_ln = ln_b.rw("vector", lambda h: h.tensor_tensor(
            sb[xln][:], sb[xln][:], sb["r_rep"][:], ALU.mult))
        xb_b[b].note_read(tok_ln)
        tok_dtx = tA_b.write("vector", lambda h: h.tensor_tensor(
            sb[tA][:], sb[tC][:], sb[xln][:], ALU.mult))
        tC_b.note_read(tok_dtx)
        ln_b.note_read(tok_dtx)
        tC_b.read("sync", (lambda d_: lambda h: h.dma_start(
            out=Tx["dt_dram"][d_ * 128:d_ * 128 + 128, :], in_=sb[tC][:]))(dt_i),
            dma=True, ch="dts")
        tA_b.read("sync", (lambda d_: lambda h: h.dma_start(
            out=Tx["dtx_dram"][d_ * 128:d_ * 128 + 128, :], in_=sb[tA][:]))(dt_i),
            dma=True, ch="dxs")


def phase_s1(nc, P, sb, Tx, prp, p):
    global Tx_G
    Tx_G = Tx
    breps = [f"brep{i}_{p}" for i in range(4)]
    creps = [f"crep{i}_{p}" for i in range(4)]
    if p == 1:
        P.step("vector", lambda h: h.tensor_scalar(
            sb["minit"][:], sb["hstate"][:], sb["carry_c"][:, 0:1], None, ALU.mult))
    bcl_b = Buf(P)
    prp_b = Buf(P)
    rep_b = {nm: Buf(P) for nm in breps + creps}
    sdt_b = [Buf(P), Buf(P)]
    sdx_b = [Buf(P), Buf(P)]
    da_b = [Buf(P), Buf(P)]
    uu_b = [Buf(P), Buf(P)]
    hh_b = [Buf(P), Buf(P)]
    y_b = Buf(P)
    hstate_b = Buf(P)
    pend = []
    yacc_pend = []
    for nb in range(4):
        for q in range(4):
            n = nb * 4 + q
            rows = [(n, breps[q])] + ([(16 + n, creps[q])] if p == 1 else [])
            for row, dst in rows:
                bcl_b.write("sync", (lambda row_: lambda h: h.dma_start(
                    out=sb["bcl"][0:1, :], in_=Tx["bc_dram"][row_:row_ + 1, :]))(row),
                    dma=True, ch="bcl")
                P.wait("tensor", bcl_b.w)
                P.waits("tensor", prp_b.readers)
                tok_rp = None
                for tc in range(4):
                    tok_rp = P.step("tensor", (lambda tc_: lambda h: h.matmul(
                        prp[:, tc_ * 512:(tc_ + 1) * 512], sb["ones_f"][0:1, :],
                        sb["bcl"][0:1, tc_ * 512:(tc_ + 1) * 512],
                        start=True, stop=True))(tc))
                bcl_b.note_read(tok_rp)
                prp_b.note_write(tok_rp)
                P.wait("vector", tok_rp)
                tok_ev = rep_b[dst].write("vector", (lambda dst_: lambda h: h.tensor_copy(
                    sb[dst_][:], prp[:, 0:TP]))(dst))
                prp_b.note_read(tok_ev)
        for dt_i in range(NDT):
            b = dt_i % 2
            sdt = f"sdt0_{p}" if b == 0 else f"sdt1_{p}"
            sdx = f"sdx0_{p}" if b == 0 else f"sdx1_{p}"
            sdt_b[b].write("sync", (lambda s_, d_: lambda h: h.dma_start(
                out=sb[s_][:], in_=Tx["dt_dram"][d_ * 128:d_ * 128 + 128, :]))(sdt, dt_i),
                dma=True, ch=f"sd{b}")
            sdx_b[b].write("sync", (lambda s_, d_: lambda h: h.dma_start(
                out=sb[s_][:], in_=Tx["dtx_dram"][d_ * 128:d_ * 128 + 128, :]))(sdx, dt_i),
                dma=True, ch=f"sx{b}")
            for q in range(4):
                n = nb * 4 + q
                b2 = q % 2
                da = f"da0_{p}" if b2 == 0 else f"da1_{p}"
                uu = f"uu0_{p}" if b2 == 0 else f"uu1_{p}"
                hh = f"hh0_{p}" if b2 == 0 else f"hh1_{p}"
                P.wait("scalar", sdt_b[b].w)
                tok_da = da_b[b2].write("scalar", (lambda da_, s_, d_, n_: lambda h:
                    h.activation(sb[da_][:], sb[s_][:], AF.Exp,
                                 scale=sb["a_c"][:, d_ * NST + n_:d_ * NST + n_ + 1]))(
                    da, sdt, dt_i, n))
                sdt_b[b].note_read(tok_da)
                P.wait("gpsimd", sdx_b[b].w)
                P.wait("gpsimd", rep_b[breps[q]].w)
                tok_u = uu_b[b2].write("gpsimd", (lambda uu_, s_, q_: lambda h:
                    h.tensor_tensor(sb[uu_][:], sb[s_][:], sb[breps[q_]][:], ALU.mult))(
                    uu, sdx, q))
                sdx_b[b].note_read(tok_u)
                rep_b[breps[q]].note_read(tok_u)
                P.waits("vector", [tok_da, tok_u])
                if p == 0:
                    tok_scan = hh_b[b2].write("vector", (lambda hh_, da_, uu_: lambda h:
                        h.tensor_tensor_scan(sb[hh_][:], sb[da_][:], sb[uu_][:],
                                             0.0, ALU.mult, ALU.add))(hh, da, uu))
                    da_b[b2].note_read(tok_scan)
                    uu_b[b2].note_read(tok_scan)
                    tok_hs = hh_b[b2].read("vector", (lambda hh_, d_, n_: lambda h:
                        h.tensor_copy(sb["hstate"][:, d_ * NST + n_:d_ * NST + n_ + 1],
                                      sb[hh_][:, TP - 1:TP]))(hh, dt_i, n))
                    hstate_b.note_write(tok_hs)
                else:
                    P.wait("vector", hstate_b.w)
                    tok_scan = hh_b[b2].write("vector", (lambda hh_, da_, uu_, d_, n_:
                        lambda h: h.tensor_tensor_scan(
                            sb[hh_][:], sb[da_][:], sb[uu_][:],
                            sb["minit"][:, d_ * NST + n_:d_ * NST + n_ + 1],
                            ALU.mult, ALU.add))(hh, da, uu, dt_i, n))
                    da_b[b2].note_read(tok_scan)
                    uu_b[b2].note_read(tok_scan)
                    P.wait("gpsimd", tok_scan)
                    P.wait("gpsimd", rep_b[creps[q]].w)
                    tok_ch = uu_b[b2].write("gpsimd",
                        (lambda hh_, q_, uu_: lambda h: h.tensor_tensor(
                            sb[uu_][:], sb[hh_][:], sb[creps[q_]][:],
                            ALU.mult))(hh, q, uu))
                    hh_b[b2].note_read(tok_ch)
                    rep_b[creps[q]].note_read(tok_ch)
                    pend.append((dt_i, q, uu, tok_ch))
                    if len(pend) > 1:
                        _flush_y(P, sb, p, y_b, uu_b, pend, yacc_pend, 1)
            if p == 1:
                yacc_pend.append((dt_i, nb))
    if p == 1:
        _flush_y(P, sb, p, y_b, uu_b, pend, yacc_pend, 0)


def _flush_y(P, sb, p, y_b, ch_b2, pend, yacc_pend, keep):
    while len(pend) > keep:
        dt_i, q, cv, tok_ch = pend.pop(0)
        P.wait("vector", tok_ch)
        if q == 0:
            if yacc_pend:
                pd, pnb = yacc_pend.pop(0)
                _store_y(P, sb, p, y_b, pd, pnb)
            tok_y = y_b.write("vector", (lambda cv_: lambda h: h.tensor_copy(
                sb[f"yacc_{p}"][:], sb[cv_][:]))(cv))
        else:
            tok_y = y_b.rw("vector", (lambda cv_: lambda h: h.tensor_tensor(
                sb[f"yacc_{p}"][:], sb[f"yacc_{p}"][:], sb[cv_][:], ALU.add))(cv))
        ch_b2[(dt_i * 4 + q) % 2].note_read(tok_y)
    if keep == 0:
        while yacc_pend:
            pd, pnb = yacc_pend.pop(0)
            _store_y(P, sb, p, y_b, pd, pnb)


def _store_y(P, sb, p, y_b, dt_i, nb):
    if nb == 0:
        y_b.read("gpsimd", (lambda d_: lambda h: h.dma_start(
            out=Tx_G["y_dram"][d_ * 128:d_ * 128 + 128, :],
            in_=sb[f"yacc_{p}"][:]))(dt_i), dma=True, ch="ydma")
    else:
        y_b.read("gpsimd", (lambda d_: lambda h: h.dma_start(
            out=Tx_G["y_dram"][d_ * 128:d_ * 128 + 128, :],
            in_=sb[f"yacc_{p}"][:], accum_op=ALU.add))(dt_i), dma=True, ch="ydma")


def phase_s2(nc, P, sb, Tx):
    xln = "xln_2"
    xb_b = [Buf(P), Buf(P)]
    y2_b = [Buf(P), Buf(P)]
    ln_b = Buf(P)
    for dt_i in range(NDT):
        b = dt_i % 2
        xb = "xt0_2" if b == 0 else "xt1_2"
        yb = "yld0" if b == 0 else "yld1"
        xb_b[b].write("sync", (lambda xb_, d_: lambda h: h.dma_start(
            out=sb[xb_][:], in_=Tx["x2T"][d_ * 128:d_ * 128 + 128, :]))(xb, dt_i),
            dma=True, ch=f"x{b}")
        y2_b[b].write("sync", (lambda yb_, d_: lambda h: h.dma_start(
            out=sb[yb_][:], in_=Tx["y_dram"][d_ * 128:d_ * 128 + 128, :]))(yb, dt_i),
            dma=True, ch=f"yl{b}")
        P.wait("vector", xb_b[b].w)
        ln_b.write("vector", (lambda xb_: lambda h: h.tensor_tensor(
            sb[xln][:], sb[xb_][:], sb["m_rep"][:], ALU.subtract))(xb))
        ln_b.rw("vector", lambda h: h.tensor_tensor(
            sb[xln][:], sb[xln][:], sb["r_rep"][:], ALU.mult))
        tok_st = ln_b.rw("vector", (lambda xb_, d_: lambda h: h.scalar_tensor_tensor(
            sb[xln][:], sb[xln][:], sb["dvec_c"][:, d_:d_ + 1], sb[xb_][:],
            ALU.mult, ALU.add))(xb, dt_i))
        xb_b[b].note_read(tok_st)
        P.wait("vector", y2_b[b].w)
        tok_xm = ln_b.rw("vector", (lambda yb_: lambda h: h.tensor_tensor(
            sb[xln][:], sb[xln][:], sb[yb_][:], ALU.add))(yb))
        y2_b[b].note_read(tok_xm)
        ln_b.read("sync", (lambda d_: lambda h: h.dma_start(
            out=Tx["xmid_dram"][d_ * 128:d_ * 128 + 128, :], in_=sb[xln][:]))(dt_i),
            dma=True, ch="xms")


def ffn_ln2(nc, P, sb, Tx, pfa, pfb):
    _stats_sweep(nc, P, sb, Tx, pfa, pfb, Tx["xmid_dram"], ("xm0", "xm1"), "fb0", "fsq")


def ffn_gateup(nc, P, sb, Tx, pfa, pfb, th):
    t0 = th * TH
    xb_b = [Buf(P), Buf(P)]
    ln_b = Buf(P)
    x2_b = Buf(P)
    for dt_i in range(NDT):
        b = dt_i % 2
        xb = f"fxm0_{th}" if b == 0 else f"fxm1_{th}"
        xb_b[b].write("sync", (lambda xb_, d_: lambda h: h.dma_start(
            out=sb[xb_][:],
            in_=Tx["xmid_dram"][d_ * 128:d_ * 128 + 128, t0:t0 + TH]))(xb, dt_i),
            dma=True, ch=f"x{b}")
        P.wait("vector", xb_b[b].w)
        ln_b.write("vector", (lambda xb_: lambda h: h.tensor_tensor(
            sb[f"fln{th}"][:], sb[xb_][:], sb["m_rep"][:, t0:t0 + TH],
            ALU.subtract))(xb))
        tok = ln_b.rw("vector", (lambda d_: lambda h: h.tensor_tensor(
            sb[f"xln2h{th}"][:, d_ * TH:(d_ + 1) * TH], sb[f"fln{th}"][:],
            sb["r_rep"][:, t0:t0 + TH], ALU.mult))(d_ := dt_i))
        xb_b[b].note_read(tok)
        x2_b.note_write(tok)
    P.barrier()

    wA_b = [Buf(P), Buf(P)]
    wB_b = [Buf(P), Buf(P)]
    g_b = [Buf(P), Buf(P)]
    u_b = [Buf(P), Buf(P)]
    sil_b, h_b = Buf(P), Buf(P)
    NTG = [list(range(g, min(g + 2, NNT))) for g in range(0, NNT, 2)]
    for gi, ntg in enumerate(NTG):
        s = gi % 2
        nw = len(ntg) * 128
        n0 = ntg[0] * 128
        cA, cB = f"wcA{s}_{th}", f"wcB{s}_{th}"
        for kt in range(32):
            fn = (lambda kt_, n0_, nw_, cA_: lambda h: h.dma_start(
                out=sb[cA_][:, kt_ * 256:kt_ * 256 + nw_],
                in_=Tx["wgt_dram"][n0_:n0_ + nw_, kt_ * 128:kt_ * 128 + 128],
                transpose=True))(kt, n0, nw, cA)
            if kt == 0:
                P.waits("scalar", wA_b[s].readers + [wA_b[s].w])
            tokA = P.dma("scalar", fn, ch=f"wca{s}")
        wA_b[s].w = tokA
        wA_b[s].readers = []
        for kt in range(32):
            fn = (lambda kt_, n0_, nw_, cB_: lambda h: h.dma_start(
                out=sb[cB_][:, kt_ * 256:kt_ * 256 + nw_],
                in_=Tx["wut_dram"][n0_:n0_ + nw_, kt_ * 128:kt_ * 128 + 128],
                transpose=True))(kt, n0, nw, cB)
            if kt == 0:
                P.waits("scalar", wB_b[s].readers + [wB_b[s].w])
            tokB = P.dma("scalar", fn, ch=f"wcb{s}")
        wB_b[s].w = tokB
        wB_b[s].readers = []
        for qi, nt in enumerate(ntg):
            oq = qi * TH
            P.wait("tensor", wA_b[s].w)
            P.waits("tensor", g_b[qi].readers)
            tok_g = None
            for kt in range(32):
                for tc in range(TH // 512):
                    fn = (lambda kt_, tc_, qi_, oq_, cA_: lambda h: h.matmul(
                        pfa[:, oq_ + tc_ * 512:oq_ + (tc_ + 1) * 512],
                        sb[cA_][:, kt_ * 256 + qi_ * 128:kt_ * 256 + qi_ * 128 + 128],
                        sb[f"xln2h{th}"][:, kt_ * TH + tc_ * 512:kt_ * TH + (tc_ + 1) * 512],
                        start=(kt_ == 0), stop=(kt_ == 31)))(kt, tc, qi, oq, cA)
                    tok_g = P.step("tensor", fn) if (kt == 31 and tc == TH // 512 - 1) \
                        else P.emit("tensor", fn)
            g_b[qi].w = tok_g
            g_b[qi].readers = []
            wA_b[s].note_read(tok_g)
            x2_b.note_read(tok_g)
            P.wait("tensor", wB_b[s].w)
            P.waits("tensor", u_b[qi].readers)
            tok_u = None
            for kt in range(32):
                for tc in range(TH // 512):
                    fn = (lambda kt_, tc_, qi_, oq_, cB_: lambda h: h.matmul(
                        pfb[:, oq_ + tc_ * 512:oq_ + (tc_ + 1) * 512],
                        sb[cB_][:, kt_ * 256 + qi_ * 128:kt_ * 256 + qi_ * 128 + 128],
                        sb[f"xln2h{th}"][:, kt_ * TH + tc_ * 512:kt_ * TH + (tc_ + 1) * 512],
                        start=(kt_ == 0), stop=(kt_ == 31)))(kt, tc, qi, oq, cB)
                    tok_u = P.step("tensor", fn) if (kt == 31 and tc == TH // 512 - 1) \
                        else P.emit("tensor", fn)
            u_b[qi].w = tok_u
            u_b[qi].readers = []
            wB_b[s].note_read(tok_u)
            x2_b.note_read(tok_u)
            P.waits("vector", h_b.readers + [h_b.w])
            for tc in range(TH // 512):
                P.wait("scalar", g_b[qi].w)
                P.waits("scalar", sil_b.readers)
                tok_sil = P.step("scalar", (lambda tc_, oq_: lambda h: h.activation(
                    sb[f"gsil{th}"][:], pfa[:, oq_ + tc_ * 512:oq_ + (tc_ + 1) * 512],
                    AF.Silu))(tc, oq))
                sil_b.w = tok_sil
                sil_b.readers = []
                g_b[qi].note_read(tok_sil)
                P.waits("vector", [tok_sil, u_b[qi].w])
                tok_h = P.step("vector", (lambda tc_, oq_: lambda h: h.tensor_tensor(
                    sb[f"hbf{th}"][:, tc_ * 512:(tc_ + 1) * 512],
                    pfb[:, oq_ + tc_ * 512:oq_ + (tc_ + 1) * 512],
                    sb[f"gsil{th}"][:], ALU.mult))(tc, oq))
                u_b[qi].note_read(tok_h)
                sil_b.note_read(tok_h)
            h_b.w = tok_h
            h_b.readers = []
            h_b.read("sync", (lambda nt_: lambda h: h.dma_start(
                out=Tx["h_dram"][nt_ * 128:nt_ * 128 + 128, t0:t0 + TH],
                in_=sb[f"hbf{th}"][:, 0:TH]))(nt), dma=True, ch="hst")


def ffn_down(nc, P, sb, Tx, pfa, pfb, th):
    t0 = th * TH
    KIG = [list(range(g, min(g + 11, NNT))) for g in range(0, NNT, 11)]
    wdL_b, wdH_b = Buf(P), Buf(P)
    hg_b = [Buf(P), Buf(P)]
    ps_b = [Buf(P) for _ in range(4)]
    oac_b = [Buf(P) for _ in range(4)]
    xm_b = [Buf(P), Buf(P)]
    o_b = Buf(P)
    pview = [pfa[:, 0:TH], pfa[:, TH:2 * TH], pfb[:, 0:TH], pfb[:, TH:2 * TH]]
    for blk in range(NDT // 4):
        dds = [4 * blk + i for i in range(4)]
        for half, (wd_b, k0, k1) in enumerate(
                ((wdL_b, 0, 44), (wdH_b, 44, NNT))):
            wnm = f"wdL{th}" if half == 0 else f"wdH{th}"
            for kt in range(k0, k1):
                fn = (lambda kt_, c0_, wnm_, k0_: lambda h: h.dma_start(
                    out=sb[wnm_][:, (kt_ - k0_) * 512:(kt_ - k0_ + 1) * 512],
                    in_=Tx["wd_dram"][kt_ * 128:kt_ * 128 + 128, c0_:c0_ + 512]))(
                    kt, dds[0] * 128, wnm, k0)
                if kt == k0:
                    P.waits("scalar", wd_b.readers + [wd_b.w])
                tok_wd = P.dma("scalar", fn, ch=("wdl" if half == 0 else "wdh"))
            wd_b.w = tok_wd
            wd_b.readers = []
        for i in range(4):
            P.waits("vector", oac_b[i].readers)
            tok_z = P.step("vector", (lambda i_: lambda h: h.memset(
                sb[f"oacc{i_}_{th}"][:], 0.0))(i))
            oac_b[i].note_write(tok_z)
        for gi, kig in enumerate(KIG):
            g2 = gi % 2
            hb = f"hg{g2}_{th}"
            hg_b[g2].write("sync", (lambda hb_, k0_: lambda h: h.dma_start(
                out=sb[hb_][:, 0:TH],
                in_=Tx["h_dram"][k0_ * 128:k0_ * 128 + 128, t0:t0 + TH]))(hb, kig[0]),
                dma=True, ch=f"hg{g2}")
            tok_h = hg_b[g2].w
            for j, kt in enumerate(kig[1:], start=1):
                tok_h = P.dma("sync", (lambda hb_, j_, kt_: lambda h: h.dma_start(
                    out=sb[hb_][:, j_ * TH:(j_ + 1) * TH],
                    in_=Tx["h_dram"][kt_ * 128:kt_ * 128 + 128, t0:t0 + TH]))(hb, j, kt),
                    ch=f"hg{g2}")
            hg_b[g2].w = tok_h
            P.wait("tensor", tok_h)
            if gi == 0:
                P.wait("tensor", wdL_b.w)
            if gi == 4:
                P.wait("tensor", wdH_b.w)
            for i in range(4):
                P.waits("tensor", ps_b[i].readers)
            tok_grp = None
            nmm = 0
            for i in range(4):
                for j, kt in enumerate(kig):
                    for tc in range(TH // 512):
                        nmm += 1
                        wnm = f"wdL{th}" if kt < 44 else f"wdH{th}"
                        kr = kt if kt < 44 else kt - 44
                        fn = (lambda i_, kr_, tc_, hb_, j_, wnm_, first, last: lambda h:
                            h.matmul(
                                pview[i_][:, tc_ * 512:(tc_ + 1) * 512],
                                sb[wnm_][:, kr_ * 512 + i_ * 128:
                                          kr_ * 512 + i_ * 128 + 128],
                                sb[hb_][:, j_ * TH + tc_ * 512:j_ * TH + (tc_ + 1) * 512],
                                start=first, stop=last))(i, kr, tc, hb, j, wnm,
                                                         j == 0, j == len(kig) - 1)
                        tok_grp = P.step("tensor", fn) \
                            if (i == 3 and j == len(kig) - 1 and tc == TH // 512 - 1) \
                            else P.emit("tensor", fn)
            hg_b[g2].note_read(tok_grp)
            (wdL_b if gi <= 3 else wdH_b).note_read(tok_grp)
            for i in range(4):
                ps_b[i].note_write(tok_grp)
            P.wait("vector", tok_grp)
            for i in range(4):
                tok_e = oac_b[i].rw("vector", (lambda i_: lambda h: h.tensor_tensor(
                    sb[f"oacc{i_}_{th}"][:], sb[f"oacc{i_}_{th}"][:], pview[i_],
                    ALU.add))(i))
                ps_b[i].note_read(tok_e)
        for i, dd in enumerate(dds):
            xb = f"dxm{i % 2}_{th}"
            xm_b[i % 2].write("sync", (lambda xb_, d_: lambda h: h.dma_start(
                out=sb[xb_][:, 0:TH],
                in_=Tx["xmid_dram"][d_ * 128:d_ * 128 + 128, t0:t0 + TH]))(xb, dd),
                dma=True, ch=f"fx{i % 2}")
            P.waits("vector", [xm_b[i % 2].w, oac_b[i].w])
            tok_o = o_b.write("vector", (lambda xb_, i_: lambda h: h.tensor_tensor(
                sb[f"oout{th}"][:, 0:TH], sb[f"oacc{i_}_{th}"][:], sb[xb_][:, 0:TH],
                ALU.add))(xb, i))
            xm_b[i % 2].note_read(tok_o)
            oac_b[i].note_read(tok_o)
            o_b.read("sync", (lambda d_: lambda h: h.dma_start(
                out=Tx["outT"][d_ * 128:d_ * 128 + 128, t0:t0 + TH],
                in_=sb[f"oout{th}"][:, 0:TH]))(dd), dma=True, ch="ost")


_PROG_CACHE = {}


def kernel(L0_attn_x, L0_attn_A_log, L0_attn_D, L0_attn_x_proj_w,
           L0_attn_dt_proj_w, L0_attn_dt_proj_b,
           L0_ffn_gate_qweight, L0_ffn_gate_scales, L0_ffn_gate_qzeros,
           L0_ffn_gate_g_idx, L0_ffn_up_qweight, L0_ffn_up_scales,
           L0_ffn_up_qzeros, L0_ffn_up_g_idx, L0_ffn_down_qweight,
           L0_ffn_down_scales, L0_ffn_down_qzeros, L0_ffn_down_g_idx):
    x = np.asarray(L0_attn_x)
    assert x.shape == (4, 4096, 4096)
    gidx = np.arange(D, dtype=np.int64) // GROUP
    didx = np.arange(INTER, dtype=np.int64) // GROUP
    assert np.array_equal(np.asarray(L0_ffn_gate_g_idx, dtype=np.int64), gidx)
    assert np.array_equal(np.asarray(L0_ffn_up_g_idx, dtype=np.int64), gidx)
    assert np.array_equal(np.asarray(L0_ffn_down_g_idx, dtype=np.int64), didx)

    if "nc" not in _PROG_CACHE:
        _PROG_CACHE["nc"] = build_program()
    nc = _PROG_CACHE["nc"]

    f32 = np.float32
    xpwT = np.ascontiguousarray(np.asarray(L0_attn_x_proj_w).T).astype(f32)
    dtwT = np.ascontiguousarray(np.asarray(L0_attn_dt_proj_w).T).astype(f32)
    dtb = np.ascontiguousarray(np.asarray(L0_attn_dt_proj_b).reshape(NDT, 128).T).astype(f32)
    alog = np.ascontiguousarray(
        np.asarray(L0_attn_A_log).reshape(NDT, 128, NST).transpose(1, 0, 2).reshape(
            128, NDT * NST)).astype(f32)
    dvec = np.ascontiguousarray(np.asarray(L0_attn_D).reshape(NDT, 128).T).astype(f32)
    shif8 = (4 * (np.arange(128, dtype=np.int32) % 8)).reshape(128, 1)
    shifr = np.ascontiguousarray(np.broadcast_to(shif8, (128, 86))).astype(np.int32)

    def zrep(qz):
        return np.ascontiguousarray(np.repeat(np.asarray(qz).T, 8, axis=0)).astype(np.int32)

    common = dict(
        xpwT=xpwT, dtwT=dtwT, dtb=dtb, alog=alog, dvec=dvec, shif8=shif8, shifr=shifr,
        gqT=np.ascontiguousarray(np.asarray(L0_ffn_gate_qweight).T).astype(np.int32),
        gsT=np.ascontiguousarray(np.asarray(L0_ffn_gate_scales).T).astype(f32),
        gzR=zrep(L0_ffn_gate_qzeros),
        uqT=np.ascontiguousarray(np.asarray(L0_ffn_up_qweight).T).astype(np.int32),
        usT=np.ascontiguousarray(np.asarray(L0_ffn_up_scales).T).astype(f32),
        uzR=zrep(L0_ffn_up_qzeros),
        dqT=np.ascontiguousarray(np.asarray(L0_ffn_down_qweight).T).astype(np.int32),
        dsT=np.ascontiguousarray(np.asarray(L0_ffn_down_scales).T).astype(f32),
        dzR=zrep(L0_ffn_down_qzeros),
    )

    in_maps = []
    for r in range(8):
        beta, H = r // 2, r % 2
        xT = np.ascontiguousarray(x[beta].T).astype(f32)
        m = dict(common)
        m["x1T"] = np.ascontiguousarray(xT[:, 0:TP])
        m["x2T"] = np.ascontiguousarray(xT[:, H * TP:(H + 1) * TP])
        m["carry"] = np.full((128, 1), float(H), f32)
        in_maps.append(m)

    res = run_bass_kernel_spmd(nc, in_maps, core_ids=list(range(8)))

    out = np.empty((4, 4096, 4096), np.float32)
    for r in range(8):
        beta, H = r // 2, r % 2
        out[beta, H * TP:(H + 1) * TP, :] = res.results[r]["outT"].T
    return out


# version 11
# speedup vs baseline: 1.3408x; 1.3408x over previous
import numpy as np
import concourse.bass as bass
import concourse.mybir as mybir
from concourse.bass_utils import run_bass_kernel_spmd
from contextlib import ExitStack

F32 = mybir.dt.float32
BF16 = mybir.dt.bfloat16
I32 = mybir.dt.int32
AF = mybir.ActivationFunctionType
ALU = mybir.AluOpType

D = 4096
TP = 2048
NST = 16
RK = 128
INTER = 11008
GROUP = 128
NDT = D // 128
NNT = INTER // 128
TH = 1024
EPS = 1e-5

ENGINES = ("sync", "scalar", "vector", "tensor", "gpsimd")
DMA_SEM = {"sync": "dS", "scalar": "dA", "gpsimd": "dG"}
ENG_SEM = {"vector": "v", "scalar": "s", "tensor": "t", "gpsimd": "g", "sync": "q"}


class Plan:
    def __init__(self):
        self.items = {e: [] for e in ENGINES}
        self.val = dict(v=0, s=0, t=0, g=0, q=0)

    SERIAL = ("vector", "scalar", "gpsimd")

    def emit(self, eng, fn, sem=None, amt=1):
        if eng in self.SERIAL:
            own = ENG_SEM[eng]
            if self.val[own] > 0:
                self.items[eng].append(("w", None, own, self.val[own], 0))
            sem = own if sem is None else sem
        if sem is not None:
            self.val[sem] += amt
            self.items[eng].append(("e", fn, sem, self.val[sem], amt))
            if eng == "tensor":
                self.items[eng].append(("w", None, sem, self.val[sem], 0))
            return (sem, self.val[sem])
        self.items[eng].append(("e", fn, None, 0, 0))
        return None

    def step(self, eng, fn):
        return self.emit(eng, fn, sem=ENG_SEM[eng])

    def dma(self, queue, fn, ch=None):
        sem = "d_" + (ch if ch is not None else DMA_SEM[queue])
        self.val[sem] = self.val.get(sem, 0) + 16
        self.items[queue].append(("e", fn, sem, self.val[sem], 16))
        return (sem, self.val[sem])

    def wait(self, eng, token):
        if token is None:
            return
        self.items[eng].append(("w", None, token[0], token[1], 0))

    def waits(self, eng, tokens):
        best = {}
        for t in tokens:
            if t is None:
                continue
            s, v = t
            if best.get(s, -1) < v:
                best[s] = v
        for s, v in best.items():
            self.items[eng].append(("w", None, s, v, 0))

    def cur(self, sem):
        return (sem, self.val[sem])

    def barrier(self):
        dma_toks = [self.cur(s) for s in list(self.val) if s.startswith("d_")]
        toks = {}
        for eng in ENGINES:
            self.waits(eng, dma_toks)
            toks[eng] = self.emit(eng, lambda h: h.nop(), sem=ENG_SEM[eng])
        for eng in ENGINES:
            self.waits(eng, [toks[o] for o in ENGINES if o != eng])

    def replay(self, eng_name, handle, sems):
        for it in self.items[eng_name]:
            if it[0] == "w":
                handle.wait_ge(sems[it[2]], it[3])
            else:
                ins = it[1](handle)
                if it[2] is not None:
                    ins.then_inc(sems[it[2]], it[4])


class Buf:

    def __init__(self, plan):
        self.P = plan
        self.w = None
        self.readers = []

    def write(self, eng, fn, sem=None, dma=False, ch=None):
        self.P.waits(eng, self.readers + [self.w])
        if dma:
            tok = self.P.dma(eng, fn, ch=ch)
        elif sem is False:
            tok = self.P.emit(eng, fn)
        else:
            tok = self.P.step(eng, fn)
        self.w = tok
        self.readers = []
        return tok

    def read(self, eng, fn, sem=None, dma=False, ch=None):
        self.P.wait(eng, self.w)
        if dma:
            tok = self.P.dma(eng, fn, ch=ch)
        elif sem is False:
            tok = self.P.emit(eng, fn)
            tok = self.P.cur(ENG_SEM[eng]) if eng != "sync" else self.P.cur("q")
        else:
            tok = self.P.step(eng, fn)
        self.readers.append(tok)
        return tok

    def rw(self, eng, fn):
        self.P.waits(eng, self.readers + [self.w])
        tok = self.P.step(eng, fn)
        self.w = tok
        self.readers = []
        return tok

    def note_read(self, tok):
        self.readers.append(tok)

    def note_write(self, tok):
        self.w = tok
        self.readers = []


def build_program():
    nc = bass.Bass("TRN2", target_bir_lowering=False)
    Tx = {}

    def din(name, shape, dtype):
        Tx[name] = nc.dram_tensor(name, shape, dtype, kind="ExternalInput").ap()

    din("x1T", [D, TP], F32)
    din("x2T", [D, TP], F32)
    din("carry", [128, 1], F32)
    din("xpwT", [D, 160], F32)
    din("dtwT", [RK, D], F32)
    din("dtb", [128, NDT], F32)
    din("alog", [128, NDT * NST], F32)
    din("dvec", [128, NDT], F32)
    din("shif8", [128, 1], I32)
    din("shifr", [128, 86], I32)
    din("gqT", [INTER, D // 8], I32)
    din("gsT", [INTER, D // GROUP], F32)
    din("gzR", [INTER, D // GROUP], I32)
    din("uqT", [INTER, D // 8], I32)
    din("usT", [INTER, D // GROUP], F32)
    din("uzR", [INTER, D // GROUP], I32)
    din("dqT", [D, INTER // 8], I32)
    din("dsT", [D, INTER // GROUP], F32)
    din("dzR", [D, INTER // GROUP], I32)
    Tx["outT"] = nc.dram_tensor("outT", [D, TP], F32, kind="ExternalOutput").ap()

    def dint(name, shape, dtype):
        Tx[name] = nc.dram_tensor(name, shape, dtype).ap()

    dint("wgt_dram", [INTER, D], BF16)
    dint("wut_dram", [INTER, D], BF16)
    dint("wdt_dram", [D, INTER], BF16)
    dint("wd_dram", [INTER, D], BF16)
    dint("dt_dram", [D, TP], F32)
    dint("dtx_dram", [D, TP], F32)
    dint("bc_dram", [32, TP], F32)
    dint("y_dram", [D, TP], F32)
    dint("xmid_dram", [D, TP], F32)
    dint("h_dram", [INTER, TP], BF16)

    P = Plan()

    with ExitStack() as ctx:
        E = ctx.enter_context
        sb = {}

        def S(stack, name, shape, dtype):
            sb[name] = stack.enter_context(nc.sbuf_tensor(name, shape, dtype))
            return sb[name]

        S(ctx, "ones_bf", [128, 1], BF16)
        S(ctx, "ones_f", [1, 128], F32)
        S(ctx, "carry_c", [128, 1], F32)
        S(ctx, "shif8_c", [128, 1], I32)
        S(ctx, "shifr_c", [128, 86], I32)
        S(ctx, "srow", [1, TP], F32)
        S(ctx, "srow2", [1, TP], F32)
        S(ctx, "srow3", [1, TP], F32)
        S(ctx, "m_rep", [128, TP], F32)
        S(ctx, "r_rep", [128, TP], F32)
        S(ctx, "hstate", [128, NDT * NST], F32)
        S(ctx, "minit", [128, NDT * NST], F32)

        S(ctx, "cvals", [1, 2], F32)
        P.emit("vector", lambda h: h.memset(sb["cvals"][0:1, 0:1], EPS))
        P.step("vector", lambda h: h.memset(sb["cvals"][0:1, 1:2], -1.0))
        nc.const_aps.register(F32, EPS, sb["cvals"][0:1, 0:1])
        nc.const_aps.register(F32, -1.0, sb["cvals"][0:1, 1:2])

        P.dma("sync", lambda h: h.dma_start(out=sb["carry_c"][:], in_=Tx["carry"]), ch="cst")
        P.dma("sync", lambda h: h.dma_start(out=sb["shif8_c"][:], in_=Tx["shif8"]), ch="cst")
        tok_c0 = P.dma("sync", lambda h: h.dma_start(out=sb["shifr_c"][:], in_=Tx["shifr"]), ch="cst")
        P.waits("vector", [tok_c0])
        P.step("vector", lambda h: h.memset(sb["ones_bf"][:], 1.0))
        P.step("vector", lambda h: h.memset(sb["ones_f"][:], 1.0))
        P.step("vector", lambda h: h.memset(sb["hstate"][:], 0.0))

        with ExitStack() as st:
            S(st, "qt0", [128, INTER // 8], I32)
            S(st, "qt1", [128, INTER // 8], I32)
            S(st, "w4", [128, INTER], I32)
            S(st, "wbf0", [128, INTER], BF16)
            S(st, "wbf1", [128, INTER], BF16)
            S(st, "zq", [128, 86], I32)
            S(st, "z1", [128, 86], F32)
            S(st, "zs", [128, 86], F32)
            S(st, "scg", [128, 86], F32)
            phase_dequant(nc, P, sb, Tx)
            P.barrier()

        with ExitStack() as st:
            S(st, "identi", [128, 128], I32)
            S(st, "identw", [128, 128], BF16)
            S(st, "wtin0", [128, 512], BF16)
            S(st, "wtin1", [128, 512], BF16)
            S(st, "wstage4", [128, 4 * D], BF16)
            with ExitStack() as stp:
                ptr = [stp.enter_context(nc.psum_tensor(f"ptr{i}", [128, 512], BF16))
                       for i in range(8)]
                phase_transpose(nc, P, sb, Tx, ptr)
            P.barrier()

        with ExitStack() as st:
            S(st, "a_c", [128, NDT * NST], F32)
            S(st, "dvec_c", [128, NDT], F32)
            sb["bcl"] = sb["srow3"]

            with ExitStack() as sta:
                S(sta, "alog_c", [128, NDT * NST], F32)
                P.dma("sync", lambda h: h.dma_start(out=sb["alog_c"][:], in_=Tx["alog"]), ch="cst")
                tok_cl = P.dma("sync", lambda h: h.dma_start(out=sb["dvec_c"][:], in_=Tx["dvec"]), ch="cst")
                P.waits("scalar", [tok_cl])
                tok_ae = P.step("scalar", lambda h: h.activation(
                    sb["a_c"][:], sb["alog_c"][:], AF.Exp))
                P.wait("vector", tok_ae)
                P.step("vector", lambda h: h.tensor_scalar(
                    sb["a_c"][:], sb["a_c"][:], -1.0, None, ALU.mult))
                P.barrier()

            for p in (0, 1):
                xsrc = Tx["x1T"] if p == 0 else Tx["x2T"]
                with ExitStack() as st0:
                    S(st0, f"xpw_bf{p}", [128, NDT * 160], BF16)
                    S(st0, f"dtw_bf{p}", [128, D], BF16)
                    S(st0, f"dtb_c{p}", [128, NDT], F32)
                    S(st0, f"xt0_{p}", [128, TP], F32)
                    S(st0, f"xt1_{p}", [128, TP], F32)
                    S(st0, f"xb0_{p}", [128, TP], BF16)
                    S(st0, f"xsq_{p}", [128, TP], BF16)
                    S(st0, f"xln_{p}", [128, TP], F32)
                    S(st0, f"tA_{p}", [128, TP], F32)
                    S(st0, f"tB_{p}", [128, TP], F32)
                    S(st0, f"tC_{p}", [128, TP], F32)
                    S(st0, f"dtin_bf{p}", [128, TP], BF16)
                    S(st0, f"xp_acc{p}", [128, TP], F32)
                    S(st0, f"bc_acc{p}", [32, TP], F32)
                    tok_dtb = P.dma("sync", (lambda p_: lambda h: h.dma_start(
                        out=sb[f"dtb_c{p_}"][:], in_=Tx["dtb"]))(p), ch="cst")
                    tok_xpw = P.dma("gpsimd", (lambda p_: lambda h: h.dma_start(
                        out=sb[f"xpw_bf{p_}"][:].rearrange("p (n r) -> p n r", r=160),
                        in_=Tx["xpwT"].rearrange("(n p) r -> p n r", p=128)))(p), ch="gcst")
                    tok_dtw = P.dma("gpsimd", (lambda p_: lambda h: h.dma_start(
                        out=sb[f"dtw_bf{p_}"][:], in_=Tx["dtwT"]))(p), ch="gcst")
                    P.waits("tensor", [tok_xpw, tok_dtw, tok_dtb])
                    P.waits("scalar", [tok_dtb])
                    with nc.psum_tensor(f"pxa{p}", [128, TP], F32) as pxa, \
                         nc.psum_tensor(f"pxb{p}", [32, TP], F32) as pxb:
                        phase_s0(nc, P, sb, Tx, pxa, pxb, p, xsrc)
                    P.barrier()
                with ExitStack() as st1:
                    for nm in ("brep0", "brep1", "brep2", "brep3",
                               "crep0", "crep1", "crep2", "crep3",
                               "sdt0", "sdt1", "sdx0", "sdx1",
                               "da0", "da1", "uu0", "uu1", "hh0", "hh1",
                               "yacc"):
                        S(st1, f"{nm}_{p}", [128, TP], F32)
                    with nc.psum_tensor(f"prp{p}", [128, TP], F32) as prp:
                        phase_s1(nc, P, sb, Tx, prp, p)
                    P.barrier()
            with ExitStack() as st2:
                S(st2, "xt0_2", [128, TP], F32)
                S(st2, "xt1_2", [128, TP], F32)
                S(st2, "yld0", [128, TP], F32)
                S(st2, "yld1", [128, TP], F32)
                S(st2, "xln_2", [128, TP], F32)
                phase_s2(nc, P, sb, Tx)
                P.barrier()

        with nc.psum_tensor("pfa", [128, TP], F32) as pfa, \
             nc.psum_tensor("pfb", [128, TP], F32) as pfb:
            with ExitStack() as st:
                S(st, "fb0", [128, TP], BF16)
                S(st, "fsq", [128, TP], BF16)
                S(st, "xm0", [128, TP], F32)
                S(st, "xm1", [128, TP], F32)
                ffn_ln2(nc, P, sb, Tx, pfa, pfb)
                P.barrier()
            for th in range(2):
                with ExitStack() as stg:
                    S(stg, f"fln{th}", [128, TH], F32)
                    S(stg, f"fxm0_{th}", [128, TH], F32)
                    S(stg, f"fxm1_{th}", [128, TH], F32)
                    S(stg, f"xln2h{th}", [128, NDT * TH], BF16)
                    S(stg, f"wcA{th}", [128, 32 * 512], BF16)
                    S(stg, f"wcB{th}", [128, 32 * 512], BF16)
                    S(stg, f"gsil{th}", [128, 4 * TH], F32)
                    S(stg, f"hbf{th}", [128, TH], BF16)
                    ffn_gateup(nc, P, sb, Tx, pfa, pfb, th)
                    P.barrier()
                with ExitStack() as std:
                    S(std, f"wdL{th}", [128, 44 * 512], BF16)
                    S(std, f"wdH{th}", [128, 42 * 512], BF16)
                    S(std, f"hg0_{th}", [128, 11 * TH], BF16)
                    S(std, f"hg1_{th}", [128, 11 * TH], BF16)
                    S(std, f"oacc0_{th}", [128, TH], F32)
                    S(std, f"oacc1_{th}", [128, TH], F32)
                    S(std, f"oacc2_{th}", [128, TH], F32)
                    S(std, f"oacc3_{th}", [128, TH], F32)
                    S(std, f"oout{th}", [128, TH], F32)
                    S(std, f"dxm0_{th}", [128, TH], F32)
                    S(std, f"dxm1_{th}", [128, TH], F32)
                    ffn_down(nc, P, sb, Tx, pfa, pfb, th)
                    P.barrier()

        P.waits("sync", [P.cur(s) for s in list(P.val) if s.startswith("d_")])

        sems = {n: E(nc.semaphore(n.replace("d_", "D"))) for n in P.val}
        block = E(nc.Block())

        @block.sync
        def _(h):
            P.replay("sync", h, sems)

        @block.scalar
        def _(h):
            P.replay("scalar", h, sems)

        @block.vector
        def _(h):
            P.replay("vector", h, sems)

        @block.tensor
        def _(h):
            P.replay("tensor", h, sems)

        @block.gpsimd
        def _(h):
            P.replay("gpsimd", h, sems)

    return nc


def phase_dequant(nc, P, sb, Tx):
    jobs = [
        ("gqT", "gsT", "gzR", "wgt_dram", INTER, D),
        ("uqT", "usT", "uzR", "wut_dram", INTER, D),
        ("dqT", "dsT", "dzR", "wdt_dram", D, INTER),
    ]
    q_b = [Buf(P), Buf(P)]
    w_b = [Buf(P), Buf(P)]
    z_b, s_b, z1_b, w4_b = Buf(P), Buf(P), Buf(P), Buf(P)
    for qn, sn, zn, wn, N, K in jobs:
        KP, G = K // 8, K // GROUP
        for nt in range(N // 128):
            b = nt % 2
            qb = "qt0" if b == 0 else "qt1"
            wb = "wbf0" if b == 0 else "wbf1"
            n0 = nt * 128
            q_b[b].write("sync", (lambda qb_, qn_, n0_, KP_: lambda h: h.dma_start(
                out=sb[qb_][:, 0:KP_], in_=Tx[qn_][n0_:n0_ + 128, :]))(qb, qn, n0, KP),
                dma=True, ch=f"q{b}")
            z_b.write("sync", (lambda zn_, n0_, G_: lambda h: h.dma_start(
                out=sb["zq"][:, 0:G_], in_=Tx[zn_][n0_:n0_ + 128, :]))(zn, n0, G),
                dma=True, ch="zs")
            s_b.write("sync", (lambda sn_, n0_, G_: lambda h: h.dma_start(
                out=sb["scg"][:, 0:G_], in_=Tx[sn_][n0_:n0_ + 128, :]))(sn, n0, G),
                dma=True, ch="zs")
            z_b.w = s_b.w
            z_b.rw("vector", (lambda G_: lambda h: h.tensor_tensor(
                sb["zq"][:, 0:G_], sb["zq"][:, 0:G_], sb["shifr_c"][:, 0:G_],
                ALU.logical_shift_right))(G))
            z_b.rw("vector", (lambda G_: lambda h: h.tensor_scalar(
                sb["zq"][:, 0:G_], sb["zq"][:, 0:G_], 15, None, ALU.bitwise_and))(G))
            tok = z1_b.write("vector", (lambda G_: lambda h: h.tensor_scalar(
                sb["z1"][:, 0:G_], sb["zq"][:, 0:G_], 1, None, ALU.add))(G))
            z_b.note_read(tok)
            P.wait("vector", s_b.w)
            tok_zs = z1_b.read("vector", (lambda G_: lambda h: h.scalar_tensor_tensor(
                sb["zs"][:, 0:G_], sb["z1"][:, 0:G_], -1.0, sb["scg"][:, 0:G_],
                ALU.mult, ALU.mult))(G))
            s_b.note_read(tok_zs)
            P.wait("vector", q_b[b].w)
            for j in range(8):
                tok = w4_b.rw("vector", (lambda j_, qb_, KP_, K_: lambda h: h.tensor_scalar(
                    sb["w4"][:, j_:K_:8], sb[qb_][:, 0:KP_], 4 * j_, 15,
                    ALU.logical_shift_right, ALU.bitwise_and))(j, qb, KP, K))
                q_b[b].note_read(tok)
            P.waits("vector", w_b[b].readers + [w_b[b].w])
            P.waits("scalar", w_b[b].readers + [w_b[b].w])
            P.waits("scalar", [w4_b.w, tok_zs])
            last_v = last_s = None
            for g in range(G):
                if g % 3 == 2:
                    fn = (lambda g_, wb_: lambda h: h.activation(
                        sb[wb_][:, g_ * GROUP:(g_ + 1) * GROUP],
                        sb["w4"][:, g_ * GROUP:(g_ + 1) * GROUP],
                        AF.Identity, scale=sb["scg"][:, g_:g_ + 1],
                        bias=sb["zs"][:, g_:g_ + 1]))(g, wb)
                    last_s = P.step("scalar", fn)
                else:
                    fn = (lambda g_, wb_: lambda h: h.tensor_scalar(
                        sb[wb_][:, g_ * GROUP:(g_ + 1) * GROUP],
                        sb["w4"][:, g_ * GROUP:(g_ + 1) * GROUP],
                        sb["z1"][:, g_:g_ + 1], sb["scg"][:, g_:g_ + 1],
                        ALU.subtract, ALU.mult))(g, wb)
                    last_v = P.step("vector", fn)
            for t in (last_v, last_s):
                w4_b.note_read(t)
                z1_b.note_read(t)
                s_b.note_read(t)
            w_b[b].w = last_v
            w_b[b].readers = []
            P.wait("sync", last_s)
            w_b[b].read("sync", (lambda wb_, wn_, n0_, K_: lambda h: h.dma_start(
                out=Tx[wn_][n0_:n0_ + 128, :], in_=sb[wb_][:, 0:K_]))(wb, wn, n0, K),
                dma=True, ch=f"ws{b}")
            w_b[b].readers.append(last_s)


def phase_transpose(nc, P, sb, Tx, ptr):
    tok_io = P.step("gpsimd", lambda h: h.iota(
        sb["identi"][:], [[1, 128]], base=0, channel_multiplier=-1))
    P.wait("vector", tok_io)
    tok_id = P.step("vector", lambda h: h.tensor_scalar(
        sb["identw"][:], sb["identi"][:], 0, None, ALU.is_equal))
    P.wait("tensor", tok_id)

    N, K = D, INTER
    l_b = [Buf(P), Buf(P)]
    ps_b = [Buf(P), Buf(P)]
    stage = Buf(P)
    kblocks = [list(range(g, min(g + 4, K // 128))) for g in range(0, K // 128, 4)]
    for kb in kblocks:
        nk = len(kb)
        k0 = kb[0] * 128
        for nt in range(N // 128):
            b = nt % 2
            lb = "wtin0" if b == 0 else "wtin1"
            l_b[b].write("sync", (lambda lb_, nt_, k0_, nk_: lambda h: h.dma_start(
                out=sb[lb_][:, 0:nk_ * 128],
                in_=Tx["wdt_dram"][nt_ * 128:nt_ * 128 + 128, k0_:k0_ + nk_ * 128]))(
                lb, nt, k0, nk), dma=True, ch=f"tl{b}")
            P.wait("tensor", l_b[b].w)
            P.waits("tensor", ps_b[b].readers)
            tok_tr = None
            for j in range(nk):
                fn = (lambda lb_, j_, b_: lambda h: h.transpose(
                    ptr[4 * b_ + j_][:, 0:128], sb[lb_][:, j_ * 128:(j_ + 1) * 128],
                    sb["identw"][:]))(lb, j, b)
                tok_tr = P.step("tensor", fn) if j == nk - 1 else P.emit("tensor", fn)
            l_b[b].note_read(tok_tr)
            ps_b[b].note_write(tok_tr)
            P.wait("scalar", tok_tr)
            if nt < 2:
                P.wait("scalar", stage.readers[-1] if stage.readers else None)
            tok_ev = None
            for j in range(nk):
                fn = (lambda j_, nt_, b_: lambda h: h.copy(
                    sb["wstage4"][:, j_ * N + nt_ * 128:j_ * N + nt_ * 128 + 128],
                    ptr[4 * b_ + j_][:, 0:128]))(j, nt, b)
                tok_ev = P.step("scalar", fn) if j == nk - 1 else P.emit("scalar", fn)
            ps_b[b].note_read(tok_ev)
            stage.note_write(tok_ev) if nt == N // 128 - 1 else None
        stage.note_write(tok_ev)
        for j, kt in enumerate(kb):
            stage.read("sync", (lambda kt_, j_: lambda h: h.dma_start(
                out=Tx["wd_dram"][kt_ * 128:kt_ * 128 + 128, :],
                in_=sb["wstage4"][:, j_ * N:(j_ + 1) * N]))(kt, j), dma=True, ch="tst")


def _stats_sweep(nc, P, sb, Tx, pfa, pfb, src_dram, xbufs, bfn, sqn):
    xb_b = [Buf(P), Buf(P)]
    bf_b, sq_b = Buf(P), Buf(P)
    row_b = Buf(P)
    ps_b = Buf(P)
    P.step("vector", lambda h: h.memset(sb["srow"][0:1, :], 0.0))
    tok0 = P.step("vector", lambda h: h.memset(sb["srow2"][0:1, :], 0.0))
    row_b.note_write(tok0)
    for dt_i in range(NDT):
        b = dt_i % 2
        xb = xbufs[b]
        xb_b[b].write("sync", (lambda xb_, d_: lambda h: h.dma_start(
            out=sb[xb_][:], in_=src_dram[d_ * 128:d_ * 128 + 128, :]))(xb, dt_i),
            dma=True, ch=f"x{b}")
        P.wait("vector", xb_b[b].w)
        tok_cp = bf_b.write("vector", (lambda xb_: lambda h: h.tensor_copy(
            sb[bfn][:], sb[xb_][:]))(xb))
        xb_b[b].note_read(tok_cp)
        P.wait("scalar", xb_b[b].w)
        tok_sq = sq_b.write("scalar", (lambda xb_: lambda h: h.activation(
            sb[sqn][:], sb[xb_][:], AF.Square))(xb))
        xb_b[b].note_read(tok_sq)
        P.waits("tensor", [tok_cp, tok_sq] + ps_b.readers)
        for tc in range(4):
            P.emit("tensor", (lambda tc_: lambda h: h.matmul(
                pfa[0:1, tc_ * 512:(tc_ + 1) * 512], sb["ones_bf"][:, 0:1],
                sb[bfn][:, tc_ * 512:(tc_ + 1) * 512],
                start=True, stop=True))(tc))
        tok_mm = None
        for tc in range(4):
            fn = (lambda tc_: lambda h: h.matmul(
                pfb[0:1, tc_ * 512:(tc_ + 1) * 512], sb["ones_bf"][:, 0:1],
                sb[sqn][:, tc_ * 512:(tc_ + 1) * 512],
                start=True, stop=True))(tc)
            tok_mm = P.step("tensor", fn) if tc == 3 else P.emit("tensor", fn)
        bf_b.note_read(tok_mm)
        sq_b.note_read(tok_mm)
        ps_b.note_write(tok_mm)
        P.wait("vector", tok_mm)
        P.emit("vector", lambda h: h.tensor_tensor(
            sb["srow"][0:1, :], sb["srow"][0:1, :], pfa[0:1, 0:TP], ALU.add))
        tok_acc = P.step("vector", lambda h: h.tensor_tensor(
            sb["srow2"][0:1, :], sb["srow2"][0:1, :], pfb[0:1, 0:TP], ALU.add))
        ps_b.note_read(tok_acc)
        row_b.note_write(tok_acc)
    P.emit("vector", lambda h: h.tensor_scalar(
        sb["srow"][0:1, :], sb["srow"][0:1, :], 1.0 / D, None, ALU.mult))
    P.emit("vector", lambda h: h.tensor_scalar(
        sb["srow2"][0:1, :], sb["srow2"][0:1, :], 1.0 / D, None, ALU.mult))
    P.emit("vector", lambda h: h.tensor_tensor(
        sb["srow3"][0:1, :], sb["srow"][0:1, :], sb["srow"][0:1, :], ALU.mult))
    tok_var = P.step("vector", lambda h: h.tensor_tensor(
        sb["srow2"][0:1, :], sb["srow2"][0:1, :], sb["srow3"][0:1, :], ALU.subtract))
    P.wait("scalar", tok_var)
    tok_sqr = P.step("scalar", lambda h: h.activation(
        sb["srow2"][0:1, :], sb["srow2"][0:1, :], AF.Sqrt, bias=EPS))
    P.wait("vector", tok_sqr)
    tok_rc = P.step("vector", lambda h: h.reciprocal(
        sb["srow2"][0:1, :], sb["srow2"][0:1, :]))
    P.waits("tensor", [tok_rc] + ps_b.readers)
    tok_rep = None
    for tc in range(4):
        tok_rep = P.step("tensor", (lambda tc_: lambda h: h.matmul(
            pfa[:, tc_ * 512:(tc_ + 1) * 512], sb["ones_f"][0:1, :],
            sb["srow"][0:1, tc_ * 512:(tc_ + 1) * 512], start=True, stop=True))(tc))
    P.wait("vector", tok_rep)
    tok_me = P.step("vector", lambda h: h.tensor_copy(sb["m_rep"][:], pfa[:, 0:TP]))
    P.wait("tensor", tok_me)
    for tc in range(4):
        tok_rep = P.step("tensor", (lambda tc_: lambda h: h.matmul(
            pfa[:, tc_ * 512:(tc_ + 1) * 512], sb["ones_f"][0:1, :],
            sb["srow2"][0:1, tc_ * 512:(tc_ + 1) * 512], start=True, stop=True))(tc))
    P.wait("vector", tok_rep)
    P.step("vector", lambda h: h.tensor_copy(sb["r_rep"][:], pfa[:, 0:TP]))


def phase_s0(nc, P, sb, Tx, pxa, pxb, p, xsrc):
    xt0, xt1 = f"xt0_{p}", f"xt1_{p}"
    xb0, xsq, xln = f"xb0_{p}", f"xsq_{p}", f"xln_{p}"
    tA, tB, tC = f"tA_{p}", f"tB_{p}", f"tC_{p}"
    dtin_bf = f"dtin_bf{p}"
    xp_acc, bc_acc = f"xp_acc{p}", f"bc_acc{p}"
    xpw_bf, dtw_bf, dtb_c = f"xpw_bf{p}", f"dtw_bf{p}", f"dtb_c{p}"
    _stats_sweep(nc, P, sb, Tx, pxa, pxb, xsrc, (xt0, xt1), xb0, xsq)
    P.barrier()

    xb_b = [Buf(P), Buf(P)]
    ln_b, lbf_b = Buf(P), Buf(P)
    acc_b, ps_b = Buf(P), Buf(P)
    P.step("vector", lambda h: h.memset(sb[xp_acc][:], 0.0))
    tok0 = P.step("vector", lambda h: h.memset(sb[bc_acc][0:32, :], 0.0))
    acc_b.note_write(tok0)
    for dt_i in range(NDT):
        b = dt_i % 2
        xb = xt0 if b == 0 else xt1
        xb_b[b].write("sync", (lambda xb_, d_: lambda h: h.dma_start(
            out=sb[xb_][:], in_=xsrc[d_ * 128:d_ * 128 + 128, :]))(xb, dt_i),
            dma=True, ch=f"x{b}")
        P.wait("vector", xb_b[b].w)
        ln_b.write("vector", (lambda xb_: lambda h: h.tensor_tensor(
            sb[xln][:], sb[xb_][:], sb["m_rep"][:], ALU.subtract))(xb))
        tok_ln = ln_b.rw("vector", lambda h: h.tensor_tensor(
            sb[xln][:], sb[xln][:], sb["r_rep"][:], ALU.mult))
        xb_b[b].note_read(tok_ln)
        tok_bf = lbf_b.write("vector", lambda h: h.tensor_copy(sb[xb0][:], sb[xln][:]))
        ln_b.note_read(tok_bf)
        P.waits("tensor", [tok_bf] + ps_b.readers)
        for tc in range(4):
            P.emit("tensor", (lambda tc_, d_: lambda h: h.matmul(
                pxa[:, tc_ * 512:(tc_ + 1) * 512],
                sb[xpw_bf][:, d_ * 160:d_ * 160 + 128],
                sb[xb0][:, tc_ * 512:(tc_ + 1) * 512],
                start=True, stop=True))(tc, dt_i))
        tok_mm = None
        for tc in range(4):
            fn = (lambda tc_, d_: lambda h: h.matmul(
                pxb[0:32, tc_ * 512:(tc_ + 1) * 512],
                sb[xpw_bf][:, d_ * 160 + 128:d_ * 160 + 160],
                sb[xb0][:, tc_ * 512:(tc_ + 1) * 512],
                start=True, stop=True))(tc, dt_i)
            tok_mm = P.step("tensor", fn) if tc == 3 else P.emit("tensor", fn)
        lbf_b.note_read(tok_mm)
        ps_b.note_write(tok_mm)
        P.wait("vector", tok_mm)
        P.emit("vector", lambda h: h.tensor_tensor(
            sb[xp_acc][:], sb[xp_acc][:], pxa[:, 0:TP], ALU.add))
        tok_acc = P.step("vector", lambda h: h.tensor_tensor(
            sb[bc_acc][0:32, :], sb[bc_acc][0:32, :], pxb[0:32, 0:TP], ALU.add))
        ps_b.note_read(tok_acc)
        acc_b.note_write(tok_acc)
    tok_dtin = P.step("vector", (lambda: lambda h: h.tensor_copy(
        sb[dtin_bf][:], sb[xp_acc][:]))())
    P.wait("sync", tok_dtin)
    P.dma("sync", lambda h: h.dma_start(
        out=Tx["bc_dram"][:, :], in_=sb[bc_acc][0:32, :]), ch="bc")
    P.barrier()

    xb_b = [Buf(P), Buf(P)]
    tA_b, tB_b, tC_b, ln_b = Buf(P), Buf(P), Buf(P), Buf(P)
    ps_b = Buf(P)
    for dt_i in range(NDT):
        b = dt_i % 2
        xb = xt0 if b == 0 else xt1
        xb_b[b].write("sync", (lambda xb_, d_: lambda h: h.dma_start(
            out=sb[xb_][:], in_=xsrc[d_ * 128:d_ * 128 + 128, :]))(xb, dt_i),
            dma=True, ch=f"x{b}")
        tok_dmm = None
        P.waits("tensor", ps_b.readers)
        for tc in range(4):
            tok_dmm = P.step("tensor", (lambda tc_, d_: lambda h: h.matmul(
                pxa[:, tc_ * 512:(tc_ + 1) * 512],
                sb[dtw_bf][:, d_ * 128:d_ * 128 + 128],
                sb[dtin_bf][:, tc_ * 512:(tc_ + 1) * 512],
                start=True, stop=True))(tc, dt_i))
        ps_b.note_write(tok_dmm)
        P.wait("scalar", tok_dmm)
        t1 = tA_b.write("scalar", (lambda d_: lambda h: h.activation(
            sb[tA][:], pxa[:, 0:TP], AF.Relu, bias=sb[dtb_c][:, d_:d_ + 1]))(dt_i))
        t2 = tB_b.write("scalar", (lambda d_: lambda h: h.activation(
            sb[tB][:], pxa[:, 0:TP], AF.Abs, bias=sb[dtb_c][:, d_:d_ + 1]))(dt_i))
        ps_b.note_read(t1)
        ps_b.note_read(t2)
        tok3 = tC_b.write("scalar", lambda h: h.activation(
            sb[tC][:], sb[tB][:], AF.Exp, scale=-1.0))
        tB_b.note_read(tok3)
        tok4 = tB_b.write("scalar", lambda h: h.activation(
            sb[tB][:], sb[tC][:], AF.Ln, bias=1.0))
        tC_b.note_read(tok4)
        P.waits("vector", [tA_b.w, tB_b.w])
        tok_dt = tC_b.write("vector", lambda h: h.tensor_tensor(
            sb[tC][:], sb[tA][:], sb[tB][:], ALU.add))
        tA_b.note_read(tok_dt)
        tB_b.note_read(tok_dt)
        P.wait("vector", xb_b[b].w)
        ln_b.write("vector", (lambda xb_: lambda h: h.tensor_tensor(
            sb[xln][:], sb[xb_][:], sb["m_rep"][:], ALU.subtract))(xb))
        tok_ln = ln_b.rw("vector", lambda h: h.tensor_tensor(
            sb[xln][:], sb[xln][:], sb["r_rep"][:], ALU.mult))
        xb_b[b].note_read(tok_ln)
        tok_dtx = tA_b.write("vector", lambda h: h.tensor_tensor(
            sb[tA][:], sb[tC][:], sb[xln][:], ALU.mult))
        tC_b.note_read(tok_dtx)
        ln_b.note_read(tok_dtx)
        tC_b.read("sync", (lambda d_: lambda h: h.dma_start(
            out=Tx["dt_dram"][d_ * 128:d_ * 128 + 128, :], in_=sb[tC][:]))(dt_i),
            dma=True, ch="dts")
        tA_b.read("sync", (lambda d_: lambda h: h.dma_start(
            out=Tx["dtx_dram"][d_ * 128:d_ * 128 + 128, :], in_=sb[tA][:]))(dt_i),
            dma=True, ch="dxs")


def phase_s1(nc, P, sb, Tx, prp, p):
    global Tx_G
    Tx_G = Tx
    breps = [f"brep{i}_{p}" for i in range(4)]
    creps = [f"crep{i}_{p}" for i in range(4)]
    if p == 1:
        P.step("vector", lambda h: h.tensor_scalar(
            sb["minit"][:], sb["hstate"][:], sb["carry_c"][:, 0:1], None, ALU.mult))
    bcl_b = Buf(P)
    prp_b = Buf(P)
    rep_b = {nm: Buf(P) for nm in breps + creps}
    sdt_b = [Buf(P), Buf(P)]
    sdx_b = [Buf(P), Buf(P)]
    da_b = [Buf(P), Buf(P)]
    uu_b = [Buf(P), Buf(P)]
    hh_b = [Buf(P), Buf(P)]
    y_b = Buf(P)
    hstate_b = Buf(P)
    pend = []
    yacc_pend = []
    for nb in range(4):
        for q in range(4):
            n = nb * 4 + q
            rows = [(n, breps[q])] + ([(16 + n, creps[q])] if p == 1 else [])
            for row, dst in rows:
                bcl_b.write("sync", (lambda row_: lambda h: h.dma_start(
                    out=sb["bcl"][0:1, :], in_=Tx["bc_dram"][row_:row_ + 1, :]))(row),
                    dma=True, ch="bcl")
                P.wait("tensor", bcl_b.w)
                P.waits("tensor", prp_b.readers)
                tok_rp = None
                for tc in range(4):
                    tok_rp = P.step("tensor", (lambda tc_: lambda h: h.matmul(
                        prp[:, tc_ * 512:(tc_ + 1) * 512], sb["ones_f"][0:1, :],
                        sb["bcl"][0:1, tc_ * 512:(tc_ + 1) * 512],
                        start=True, stop=True))(tc))
                bcl_b.note_read(tok_rp)
                prp_b.note_write(tok_rp)
                P.wait("vector", tok_rp)
                tok_ev = rep_b[dst].write("vector", (lambda dst_: lambda h: h.tensor_copy(
                    sb[dst_][:], prp[:, 0:TP]))(dst))
                prp_b.note_read(tok_ev)
        for dt_i in range(NDT):
            b = dt_i % 2
            sdt = f"sdt0_{p}" if b == 0 else f"sdt1_{p}"
            sdx = f"sdx0_{p}" if b == 0 else f"sdx1_{p}"
            sdt_b[b].write("sync", (lambda s_, d_: lambda h: h.dma_start(
                out=sb[s_][:], in_=Tx["dt_dram"][d_ * 128:d_ * 128 + 128, :]))(sdt, dt_i),
                dma=True, ch=f"sd{b}")
            sdx_b[b].write("sync", (lambda s_, d_: lambda h: h.dma_start(
                out=sb[s_][:], in_=Tx["dtx_dram"][d_ * 128:d_ * 128 + 128, :]))(sdx, dt_i),
                dma=True, ch=f"sx{b}")
            for q in range(4):
                n = nb * 4 + q
                b2 = q % 2
                da = f"da0_{p}" if b2 == 0 else f"da1_{p}"
                uu = f"uu0_{p}" if b2 == 0 else f"uu1_{p}"
                hh = f"hh0_{p}" if b2 == 0 else f"hh1_{p}"
                P.wait("scalar", sdt_b[b].w)
                tok_da = da_b[b2].write("scalar", (lambda da_, s_, d_, n_: lambda h:
                    h.activation(sb[da_][:], sb[s_][:], AF.Exp,
                                 scale=sb["a_c"][:, d_ * NST + n_:d_ * NST + n_ + 1]))(
                    da, sdt, dt_i, n))
                sdt_b[b].note_read(tok_da)
                P.wait("gpsimd", sdx_b[b].w)
                P.wait("gpsimd", rep_b[breps[q]].w)
                tok_u = uu_b[b2].write("gpsimd", (lambda uu_, s_, q_: lambda h:
                    h.tensor_tensor(sb[uu_][:], sb[s_][:], sb[breps[q_]][:], ALU.mult))(
                    uu, sdx, q))
                sdx_b[b].note_read(tok_u)
                rep_b[breps[q]].note_read(tok_u)
                P.waits("vector", [tok_da, tok_u])
                if p == 0:
                    tok_scan = hh_b[b2].write("vector", (lambda hh_, da_, uu_: lambda h:
                        h.tensor_tensor_scan(sb[hh_][:], sb[da_][:], sb[uu_][:],
                                             0.0, ALU.mult, ALU.add))(hh, da, uu))
                    da_b[b2].note_read(tok_scan)
                    uu_b[b2].note_read(tok_scan)
                    tok_hs = hh_b[b2].read("vector", (lambda hh_, d_, n_: lambda h:
                        h.tensor_copy(sb["hstate"][:, d_ * NST + n_:d_ * NST + n_ + 1],
                                      sb[hh_][:, TP - 1:TP]))(hh, dt_i, n))
                    hstate_b.note_write(tok_hs)
                else:
                    P.wait("vector", hstate_b.w)
                    tok_scan = hh_b[b2].write("vector", (lambda hh_, da_, uu_, d_, n_:
                        lambda h: h.tensor_tensor_scan(
                            sb[hh_][:], sb[da_][:], sb[uu_][:],
                            sb["minit"][:, d_ * NST + n_:d_ * NST + n_ + 1],
                            ALU.mult, ALU.add))(hh, da, uu, dt_i, n))
                    da_b[b2].note_read(tok_scan)
                    uu_b[b2].note_read(tok_scan)
                    P.wait("gpsimd", tok_scan)
                    P.wait("gpsimd", rep_b[creps[q]].w)
                    tok_ch = uu_b[b2].write("gpsimd",
                        (lambda hh_, q_, uu_: lambda h: h.tensor_tensor(
                            sb[uu_][:], sb[hh_][:], sb[creps[q_]][:],
                            ALU.mult))(hh, q, uu))
                    hh_b[b2].note_read(tok_ch)
                    rep_b[creps[q]].note_read(tok_ch)
                    pend.append((dt_i, q, uu, tok_ch))
                    if len(pend) > 1:
                        _flush_y(P, sb, p, y_b, uu_b, pend, yacc_pend, 1)
            if p == 1:
                yacc_pend.append((dt_i, nb))
    if p == 1:
        _flush_y(P, sb, p, y_b, uu_b, pend, yacc_pend, 0)


def _flush_y(P, sb, p, y_b, ch_b2, pend, yacc_pend, keep):
    while len(pend) > keep:
        dt_i, q, cv, tok_ch = pend.pop(0)
        P.wait("vector", tok_ch)
        if q == 0:
            if yacc_pend:
                pd, pnb = yacc_pend.pop(0)
                _store_y(P, sb, p, y_b, pd, pnb)
            tok_y = y_b.write("vector", (lambda cv_: lambda h: h.tensor_copy(
                sb[f"yacc_{p}"][:], sb[cv_][:]))(cv))
        else:
            tok_y = y_b.rw("vector", (lambda cv_: lambda h: h.tensor_tensor(
                sb[f"yacc_{p}"][:], sb[f"yacc_{p}"][:], sb[cv_][:], ALU.add))(cv))
        ch_b2[(dt_i * 4 + q) % 2].note_read(tok_y)
    if keep == 0:
        while yacc_pend:
            pd, pnb = yacc_pend.pop(0)
            _store_y(P, sb, p, y_b, pd, pnb)


def _store_y(P, sb, p, y_b, dt_i, nb):
    if nb == 0:
        y_b.read("gpsimd", (lambda d_: lambda h: h.dma_start(
            out=Tx_G["y_dram"][d_ * 128:d_ * 128 + 128, :],
            in_=sb[f"yacc_{p}"][:]))(dt_i), dma=True, ch="ydma")
    else:
        y_b.read("gpsimd", (lambda d_: lambda h: h.dma_start(
            out=Tx_G["y_dram"][d_ * 128:d_ * 128 + 128, :],
            in_=sb[f"yacc_{p}"][:], accum_op=ALU.add))(dt_i), dma=True, ch="ydma")


def phase_s2(nc, P, sb, Tx):
    xln = "xln_2"
    xb_b = [Buf(P), Buf(P)]
    y2_b = [Buf(P), Buf(P)]
    ln_b = Buf(P)
    for dt_i in range(NDT):
        b = dt_i % 2
        xb = "xt0_2" if b == 0 else "xt1_2"
        yb = "yld0" if b == 0 else "yld1"
        xb_b[b].write("sync", (lambda xb_, d_: lambda h: h.dma_start(
            out=sb[xb_][:], in_=Tx["x2T"][d_ * 128:d_ * 128 + 128, :]))(xb, dt_i),
            dma=True, ch=f"x{b}")
        y2_b[b].write("sync", (lambda yb_, d_: lambda h: h.dma_start(
            out=sb[yb_][:], in_=Tx["y_dram"][d_ * 128:d_ * 128 + 128, :]))(yb, dt_i),
            dma=True, ch=f"yl{b}")
        P.wait("vector", xb_b[b].w)
        ln_b.write("vector", (lambda xb_: lambda h: h.tensor_tensor(
            sb[xln][:], sb[xb_][:], sb["m_rep"][:], ALU.subtract))(xb))
        ln_b.rw("vector", lambda h: h.tensor_tensor(
            sb[xln][:], sb[xln][:], sb["r_rep"][:], ALU.mult))
        tok_st = ln_b.rw("vector", (lambda xb_, d_: lambda h: h.scalar_tensor_tensor(
            sb[xln][:], sb[xln][:], sb["dvec_c"][:, d_:d_ + 1], sb[xb_][:],
            ALU.mult, ALU.add))(xb, dt_i))
        xb_b[b].note_read(tok_st)
        P.wait("vector", y2_b[b].w)
        tok_xm = ln_b.rw("vector", (lambda yb_: lambda h: h.tensor_tensor(
            sb[xln][:], sb[xln][:], sb[yb_][:], ALU.add))(yb))
        y2_b[b].note_read(tok_xm)
        ln_b.read("sync", (lambda d_: lambda h: h.dma_start(
            out=Tx["xmid_dram"][d_ * 128:d_ * 128 + 128, :], in_=sb[xln][:]))(dt_i),
            dma=True, ch="xms")


def ffn_ln2(nc, P, sb, Tx, pfa, pfb):
    _stats_sweep(nc, P, sb, Tx, pfa, pfb, Tx["xmid_dram"], ("xm0", "xm1"), "fb0", "fsq")


def ffn_gateup(nc, P, sb, Tx, pfa, pfb, th):
    t0 = th * TH
    xb_b = [Buf(P), Buf(P)]
    ln_b = Buf(P)
    x2_b = Buf(P)
    for dt_i in range(NDT):
        b = dt_i % 2
        xb = f"fxm0_{th}" if b == 0 else f"fxm1_{th}"
        xb_b[b].write("sync", (lambda xb_, d_: lambda h: h.dma_start(
            out=sb[xb_][:],
            in_=Tx["xmid_dram"][d_ * 128:d_ * 128 + 128, t0:t0 + TH]))(xb, dt_i),
            dma=True, ch=f"x{b}")
        P.wait("vector", xb_b[b].w)
        ln_b.write("vector", (lambda xb_: lambda h: h.tensor_tensor(
            sb[f"fln{th}"][:], sb[xb_][:], sb["m_rep"][:, t0:t0 + TH],
            ALU.subtract))(xb))
        tok = ln_b.rw("vector", (lambda d_: lambda h: h.tensor_tensor(
            sb[f"xln2h{th}"][:, d_ * TH:(d_ + 1) * TH], sb[f"fln{th}"][:],
            sb["r_rep"][:, t0:t0 + TH], ALU.mult))(d_ := dt_i))
        xb_b[b].note_read(tok)
        x2_b.note_write(tok)
    P.barrier()

    wA_b, wB_b = Buf(P), Buf(P)
    g_b = [Buf(P) for _ in range(4)]
    u_b = [Buf(P) for _ in range(4)]
    sil_b = [Buf(P) for _ in range(4)]
    h_b = Buf(P)
    pq = [pfa[:, 0:TH], pfa[:, TH:2 * TH], pfb[:, 0:TH], pfb[:, TH:2 * TH]]
    NTG = [list(range(g, min(g + 4, NNT))) for g in range(0, NNT, 4)]
    for ntg in NTG:
        nw = len(ntg) * 128
        n0 = ntg[0] * 128
        for kt in range(32):
            fn = (lambda kt_, n0_, nw_: lambda h: h.dma_start(
                out=sb[f"wcA{th}"][:, kt_ * 512:kt_ * 512 + nw_],
                in_=Tx["wgt_dram"][n0_:n0_ + nw_, kt_ * 128:kt_ * 128 + 128],
                transpose=True))(kt, n0, nw)
            if kt == 0:
                P.waits("scalar", wA_b.readers + [wA_b.w])
            tokA = P.dma("scalar", fn, ch="wca")
        wA_b.w = tokA
        wA_b.readers = []
        for qi, nt in enumerate(ntg):
            P.wait("tensor", wA_b.w)
            P.waits("tensor", g_b[qi].readers + u_b[qi].readers)
            tok_g = None
            for kt in range(32):
                for tc in range(TH // 512):
                    fn = (lambda kt_, tc_, qi_: lambda h: h.matmul(
                        pq[qi_][:, tc_ * 512:(tc_ + 1) * 512],
                        sb[f"wcA{th}"][:, kt_ * 512 + qi_ * 128:kt_ * 512 + qi_ * 128 + 128],
                        sb[f"xln2h{th}"][:, kt_ * TH + tc_ * 512:kt_ * TH + (tc_ + 1) * 512],
                        start=(kt_ == 0), stop=(kt_ == 31)))(kt, tc, qi)
                    tok_g = P.step("tensor", fn) if (kt == 31 and tc == TH // 512 - 1) \
                        else P.emit("tensor", fn)
            g_b[qi].w = tok_g
            g_b[qi].readers = []
            wA_b.note_read(tok_g)
            x2_b.note_read(tok_g)
        for kt in range(32):
            fn = (lambda kt_, n0_, nw_: lambda h: h.dma_start(
                out=sb[f"wcB{th}"][:, kt_ * 512:kt_ * 512 + nw_],
                in_=Tx["wut_dram"][n0_:n0_ + nw_, kt_ * 128:kt_ * 128 + 128],
                transpose=True))(kt, n0, nw)
            if kt == 0:
                P.waits("scalar", wB_b.readers + [wB_b.w])
            tokB = P.dma("scalar", fn, ch="wcb")
        wB_b.w = tokB
        wB_b.readers = []
        for qi in range(len(ntg)):
            P.wait("scalar", g_b[qi].w)
            P.waits("scalar", sil_b[qi].readers)
            tok_sil = None
            for tc in range(TH // 512):
                tok_sil = P.step("scalar", (lambda tc_, qi_: lambda h: h.activation(
                    sb[f"gsil{th}"][:, qi_ * TH + tc_ * 512:qi_ * TH + (tc_ + 1) * 512],
                    pq[qi_][:, tc_ * 512:(tc_ + 1) * 512], AF.Silu))(tc, qi))
            sil_b[qi].w = tok_sil
            sil_b[qi].readers = []
            g_b[qi].note_read(tok_sil)
        for qi, nt in enumerate(ntg):
            P.wait("tensor", wB_b.w)
            P.wait("tensor", sil_b[qi].w)
            P.waits("tensor", u_b[qi].readers)
            tok_u = None
            for kt in range(32):
                for tc in range(TH // 512):
                    fn = (lambda kt_, tc_, qi_: lambda h: h.matmul(
                        pq[qi_][:, tc_ * 512:(tc_ + 1) * 512],
                        sb[f"wcB{th}"][:, kt_ * 512 + qi_ * 128:kt_ * 512 + qi_ * 128 + 128],
                        sb[f"xln2h{th}"][:, kt_ * TH + tc_ * 512:kt_ * TH + (tc_ + 1) * 512],
                        start=(kt_ == 0), stop=(kt_ == 31)))(kt, tc, qi)
                    tok_u = P.step("tensor", fn) if (kt == 31 and tc == TH // 512 - 1) \
                        else P.emit("tensor", fn)
            u_b[qi].w = tok_u
            u_b[qi].readers = []
            wB_b.note_read(tok_u)
            x2_b.note_read(tok_u)
        for qi, nt in enumerate(ntg):
            P.waits("vector", h_b.readers + [h_b.w])
            tok_h = None
            for tc in range(TH // 512):
                P.waits("vector", [u_b[qi].w, sil_b[qi].w])
                tok_h = P.step("vector", (lambda tc_, qi_: lambda h: h.tensor_tensor(
                    sb[f"hbf{th}"][:, tc_ * 512:(tc_ + 1) * 512],
                    pq[qi_][:, tc_ * 512:(tc_ + 1) * 512],
                    sb[f"gsil{th}"][:, qi_ * TH + tc_ * 512:qi_ * TH + (tc_ + 1) * 512],
                    ALU.mult))(tc, qi))
                u_b[qi].note_read(tok_h)
                sil_b[qi].note_read(tok_h)
            h_b.w = tok_h
            h_b.readers = []
            h_b.read("sync", (lambda nt_: lambda h: h.dma_start(
                out=Tx["h_dram"][nt_ * 128:nt_ * 128 + 128, t0:t0 + TH],
                in_=sb[f"hbf{th}"][:, 0:TH]))(nt), dma=True, ch="hst")


def ffn_down(nc, P, sb, Tx, pfa, pfb, th):
    t0 = th * TH
    KIG = [list(range(g, min(g + 11, NNT))) for g in range(0, NNT, 11)]
    wdL_b, wdH_b = Buf(P), Buf(P)
    hg_b = [Buf(P), Buf(P)]
    ps_b = [Buf(P) for _ in range(4)]
    oac_b = [Buf(P) for _ in range(4)]
    xm_b = [Buf(P), Buf(P)]
    o_b = Buf(P)
    pview = [pfa[:, 0:TH], pfa[:, TH:2 * TH], pfb[:, 0:TH], pfb[:, TH:2 * TH]]
    for blk in range(NDT // 4):
        dds = [4 * blk + i for i in range(4)]
        for half, (wd_b, k0, k1) in enumerate(
                ((wdL_b, 0, 44), (wdH_b, 44, NNT))):
            wnm = f"wdL{th}" if half == 0 else f"wdH{th}"
            for kt in range(k0, k1):
                fn = (lambda kt_, c0_, wnm_, k0_: lambda h: h.dma_start(
                    out=sb[wnm_][:, (kt_ - k0_) * 512:(kt_ - k0_ + 1) * 512],
                    in_=Tx["wd_dram"][kt_ * 128:kt_ * 128 + 128, c0_:c0_ + 512]))(
                    kt, dds[0] * 128, wnm, k0)
                if kt == k0:
                    P.waits("scalar", wd_b.readers + [wd_b.w])
                tok_wd = P.dma("scalar", fn, ch=("wdl" if half == 0 else "wdh"))
            wd_b.w = tok_wd
            wd_b.readers = []
        for i in range(4):
            P.waits("vector", oac_b[i].readers)
            tok_z = P.step("vector", (lambda i_: lambda h: h.memset(
                sb[f"oacc{i_}_{th}"][:], 0.0))(i))
            oac_b[i].note_write(tok_z)
        for gi, kig in enumerate(KIG):
            g2 = gi % 2
            hb = f"hg{g2}_{th}"
            hg_b[g2].write("sync", (lambda hb_, k0_: lambda h: h.dma_start(
                out=sb[hb_][:, 0:TH],
                in_=Tx["h_dram"][k0_ * 128:k0_ * 128 + 128, t0:t0 + TH]))(hb, kig[0]),
                dma=True, ch=f"hg{g2}")
            tok_h = hg_b[g2].w
            for j, kt in enumerate(kig[1:], start=1):
                tok_h = P.dma("sync", (lambda hb_, j_, kt_: lambda h: h.dma_start(
                    out=sb[hb_][:, j_ * TH:(j_ + 1) * TH],
                    in_=Tx["h_dram"][kt_ * 128:kt_ * 128 + 128, t0:t0 + TH]))(hb, j, kt),
                    ch=f"hg{g2}")
            hg_b[g2].w = tok_h
            P.wait("tensor", tok_h)
            if gi == 0:
                P.wait("tensor", wdL_b.w)
            if gi == 4:
                P.wait("tensor", wdH_b.w)
            for i in range(4):
                P.waits("tensor", ps_b[i].readers)
            tok_grp = None
            nmm = 0
            for i in range(4):
                for j, kt in enumerate(kig):
                    for tc in range(TH // 512):
                        nmm += 1
                        wnm = f"wdL{th}" if kt < 44 else f"wdH{th}"
                        kr = kt if kt < 44 else kt - 44
                        fn = (lambda i_, kr_, tc_, hb_, j_, wnm_, first, last: lambda h:
                            h.matmul(
                                pview[i_][:, tc_ * 512:(tc_ + 1) * 512],
                                sb[wnm_][:, kr_ * 512 + i_ * 128:
                                          kr_ * 512 + i_ * 128 + 128],
                                sb[hb_][:, j_ * TH + tc_ * 512:j_ * TH + (tc_ + 1) * 512],
                                start=first, stop=last))(i, kr, tc, hb, j, wnm,
                                                         j == 0, j == len(kig) - 1)
                        tok_grp = P.step("tensor", fn) \
                            if (i == 3 and j == len(kig) - 1 and tc == TH // 512 - 1) \
                            else P.emit("tensor", fn)
            hg_b[g2].note_read(tok_grp)
            (wdL_b if gi <= 3 else wdH_b).note_read(tok_grp)
            for i in range(4):
                ps_b[i].note_write(tok_grp)
            P.wait("vector", tok_grp)
            for i in range(4):
                tok_e = oac_b[i].rw("vector", (lambda i_: lambda h: h.tensor_tensor(
                    sb[f"oacc{i_}_{th}"][:], sb[f"oacc{i_}_{th}"][:], pview[i_],
                    ALU.add))(i))
                ps_b[i].note_read(tok_e)
        for i, dd in enumerate(dds):
            xb = f"dxm{i % 2}_{th}"
            xm_b[i % 2].write("sync", (lambda xb_, d_: lambda h: h.dma_start(
                out=sb[xb_][:, 0:TH],
                in_=Tx["xmid_dram"][d_ * 128:d_ * 128 + 128, t0:t0 + TH]))(xb, dd),
                dma=True, ch=f"fx{i % 2}")
            P.waits("vector", [xm_b[i % 2].w, oac_b[i].w])
            tok_o = o_b.write("vector", (lambda xb_, i_: lambda h: h.tensor_tensor(
                sb[f"oout{th}"][:, 0:TH], sb[f"oacc{i_}_{th}"][:], sb[xb_][:, 0:TH],
                ALU.add))(xb, i))
            xm_b[i % 2].note_read(tok_o)
            oac_b[i].note_read(tok_o)
            o_b.read("sync", (lambda d_: lambda h: h.dma_start(
                out=Tx["outT"][d_ * 128:d_ * 128 + 128, t0:t0 + TH],
                in_=sb[f"oout{th}"][:, 0:TH]))(dd), dma=True, ch="ost")


_PROG_CACHE = {}


def kernel(L0_attn_x, L0_attn_A_log, L0_attn_D, L0_attn_x_proj_w,
           L0_attn_dt_proj_w, L0_attn_dt_proj_b,
           L0_ffn_gate_qweight, L0_ffn_gate_scales, L0_ffn_gate_qzeros,
           L0_ffn_gate_g_idx, L0_ffn_up_qweight, L0_ffn_up_scales,
           L0_ffn_up_qzeros, L0_ffn_up_g_idx, L0_ffn_down_qweight,
           L0_ffn_down_scales, L0_ffn_down_qzeros, L0_ffn_down_g_idx):
    x = np.asarray(L0_attn_x)
    assert x.shape == (4, 4096, 4096)
    gidx = np.arange(D, dtype=np.int64) // GROUP
    didx = np.arange(INTER, dtype=np.int64) // GROUP
    assert np.array_equal(np.asarray(L0_ffn_gate_g_idx, dtype=np.int64), gidx)
    assert np.array_equal(np.asarray(L0_ffn_up_g_idx, dtype=np.int64), gidx)
    assert np.array_equal(np.asarray(L0_ffn_down_g_idx, dtype=np.int64), didx)

    if "nc" not in _PROG_CACHE:
        _PROG_CACHE["nc"] = build_program()
    nc = _PROG_CACHE["nc"]

    f32 = np.float32
    xpwT = np.ascontiguousarray(np.asarray(L0_attn_x_proj_w).T).astype(f32)
    dtwT = np.ascontiguousarray(np.asarray(L0_attn_dt_proj_w).T).astype(f32)
    dtb = np.ascontiguousarray(np.asarray(L0_attn_dt_proj_b).reshape(NDT, 128).T).astype(f32)
    alog = np.ascontiguousarray(
        np.asarray(L0_attn_A_log).reshape(NDT, 128, NST).transpose(1, 0, 2).reshape(
            128, NDT * NST)).astype(f32)
    dvec = np.ascontiguousarray(np.asarray(L0_attn_D).reshape(NDT, 128).T).astype(f32)
    shif8 = (4 * (np.arange(128, dtype=np.int32) % 8)).reshape(128, 1)
    shifr = np.ascontiguousarray(np.broadcast_to(shif8, (128, 86))).astype(np.int32)

    def zrep(qz):
        return np.ascontiguousarray(np.repeat(np.asarray(qz).T, 8, axis=0)).astype(np.int32)

    common = dict(
        xpwT=xpwT, dtwT=dtwT, dtb=dtb, alog=alog, dvec=dvec, shif8=shif8, shifr=shifr,
        gqT=np.ascontiguousarray(np.asarray(L0_ffn_gate_qweight).T).astype(np.int32),
        gsT=np.ascontiguousarray(np.asarray(L0_ffn_gate_scales).T).astype(f32),
        gzR=zrep(L0_ffn_gate_qzeros),
        uqT=np.ascontiguousarray(np.asarray(L0_ffn_up_qweight).T).astype(np.int32),
        usT=np.ascontiguousarray(np.asarray(L0_ffn_up_scales).T).astype(f32),
        uzR=zrep(L0_ffn_up_qzeros),
        dqT=np.ascontiguousarray(np.asarray(L0_ffn_down_qweight).T).astype(np.int32),
        dsT=np.ascontiguousarray(np.asarray(L0_ffn_down_scales).T).astype(f32),
        dzR=zrep(L0_ffn_down_qzeros),
    )

    in_maps = []
    for r in range(8):
        beta, H = r // 2, r % 2
        xT = np.ascontiguousarray(x[beta].T).astype(f32)
        m = dict(common)
        m["x1T"] = np.ascontiguousarray(xT[:, 0:TP])
        m["x2T"] = np.ascontiguousarray(xT[:, H * TP:(H + 1) * TP])
        m["carry"] = np.full((128, 1), float(H), f32)
        in_maps.append(m)

    res = run_bass_kernel_spmd(nc, in_maps, core_ids=list(range(8)))

    out = np.empty((4, 4096, 4096), np.float32)
    for r in range(8):
        beta, H = r // 2, r % 2
        out[beta, H * TP:(H + 1) * TP, :] = res.results[r]["outT"].T
    return out


# version 13 (final)
# speedup vs baseline: 1.3827x; 1.0312x over previous
import numpy as np
import concourse.bass as bass
import concourse.mybir as mybir
from concourse.bass_utils import run_bass_kernel_spmd
from contextlib import ExitStack

F32 = mybir.dt.float32
BF16 = mybir.dt.bfloat16
I32 = mybir.dt.int32
AF = mybir.ActivationFunctionType
ALU = mybir.AluOpType

D = 4096
TP = 2048
NST = 16
RK = 128
INTER = 11008
GROUP = 128
NDT = D // 128
NNT = INTER // 128
TH = 1024
EPS = 1e-5

ENGINES = ("sync", "scalar", "vector", "tensor", "gpsimd")
DMA_SEM = {"sync": "dS", "scalar": "dA", "gpsimd": "dG"}
ENG_SEM = {"vector": "v", "scalar": "s", "tensor": "t", "gpsimd": "g", "sync": "q"}


class Plan:
    def __init__(self):
        self.items = {e: [] for e in ENGINES}
        self.val = dict(v=0, s=0, t=0, g=0, q=0)

    SERIAL = ("vector", "scalar", "gpsimd")

    def emit(self, eng, fn, sem=None, amt=1):
        if eng in self.SERIAL:
            own = ENG_SEM[eng]
            if self.val[own] > 0:
                self.items[eng].append(("w", None, own, self.val[own], 0))
            sem = own if sem is None else sem
        if sem is not None:
            self.val[sem] += amt
            self.items[eng].append(("e", fn, sem, self.val[sem], amt))
            if eng == "tensor":
                self.items[eng].append(("w", None, sem, self.val[sem], 0))
            return (sem, self.val[sem])
        self.items[eng].append(("e", fn, None, 0, 0))
        return None

    def step(self, eng, fn):
        return self.emit(eng, fn, sem=ENG_SEM[eng])

    def dma(self, queue, fn, ch=None):
        sem = "d_" + (ch if ch is not None else DMA_SEM[queue])
        self.val[sem] = self.val.get(sem, 0) + 16
        self.items[queue].append(("e", fn, sem, self.val[sem], 16))
        return (sem, self.val[sem])

    def wait(self, eng, token):
        if token is None:
            return
        self.items[eng].append(("w", None, token[0], token[1], 0))

    def waits(self, eng, tokens):
        best = {}
        for t in tokens:
            if t is None:
                continue
            s, v = t
            if best.get(s, -1) < v:
                best[s] = v
        for s, v in best.items():
            self.items[eng].append(("w", None, s, v, 0))

    def cur(self, sem):
        return (sem, self.val[sem])

    def barrier(self):
        dma_toks = [self.cur(s) for s in list(self.val) if s.startswith("d_")]
        toks = {}
        for eng in ENGINES:
            self.waits(eng, dma_toks)
            toks[eng] = self.emit(eng, lambda h: h.nop(), sem=ENG_SEM[eng])
        for eng in ENGINES:
            self.waits(eng, [toks[o] for o in ENGINES if o != eng])

    def replay(self, eng_name, handle, sems):
        for it in self.items[eng_name]:
            if it[0] == "w":
                handle.wait_ge(sems[it[2]], it[3])
            else:
                ins = it[1](handle)
                if it[2] is not None:
                    ins.then_inc(sems[it[2]], it[4])


class Buf:

    def __init__(self, plan):
        self.P = plan
        self.w = None
        self.readers = []

    def write(self, eng, fn, sem=None, dma=False, ch=None):
        self.P.waits(eng, self.readers + [self.w])
        if dma:
            tok = self.P.dma(eng, fn, ch=ch)
        elif sem is False:
            tok = self.P.emit(eng, fn)
        else:
            tok = self.P.step(eng, fn)
        self.w = tok
        self.readers = []
        return tok

    def read(self, eng, fn, sem=None, dma=False, ch=None):
        self.P.wait(eng, self.w)
        if dma:
            tok = self.P.dma(eng, fn, ch=ch)
        elif sem is False:
            tok = self.P.emit(eng, fn)
            tok = self.P.cur(ENG_SEM[eng]) if eng != "sync" else self.P.cur("q")
        else:
            tok = self.P.step(eng, fn)
        self.readers.append(tok)
        return tok

    def rw(self, eng, fn):
        self.P.waits(eng, self.readers + [self.w])
        tok = self.P.step(eng, fn)
        self.w = tok
        self.readers = []
        return tok

    def note_read(self, tok):
        self.readers.append(tok)

    def note_write(self, tok):
        self.w = tok
        self.readers = []


def build_program():
    nc = bass.Bass("TRN2", target_bir_lowering=False)
    Tx = {}

    def din(name, shape, dtype):
        Tx[name] = nc.dram_tensor(name, shape, dtype, kind="ExternalInput").ap()

    din("x1T", [D, TP], F32)
    din("x2T", [D, TP], F32)
    din("carry", [128, 1], F32)
    din("xpwT", [D, 160], F32)
    din("dtwT", [RK, D], F32)
    din("dtb", [128, NDT], F32)
    din("alog", [128, NDT * NST], F32)
    din("dvec", [128, NDT], F32)
    din("shif8", [128, 1], I32)
    din("shifr", [128, 86], I32)
    din("gqT", [INTER, D // 8], I32)
    din("gsT", [INTER, D // GROUP], F32)
    din("gzR", [INTER, D // GROUP], I32)
    din("uqT", [INTER, D // 8], I32)
    din("usT", [INTER, D // GROUP], F32)
    din("uzR", [INTER, D // GROUP], I32)
    din("dqT", [D, INTER // 8], I32)
    din("dsT", [D, INTER // GROUP], F32)
    din("dzR", [D, INTER // GROUP], I32)
    Tx["outT"] = nc.dram_tensor("outT", [D, TP], F32, kind="ExternalOutput").ap()

    def dint(name, shape, dtype):
        Tx[name] = nc.dram_tensor(name, shape, dtype).ap()

    dint("wgt_dram", [INTER, D], BF16)
    dint("wut_dram", [INTER, D], BF16)
    dint("wdt_dram", [D, INTER], BF16)
    dint("wd_dram", [INTER, D], BF16)
    dint("dt_dram", [D, TP], F32)
    dint("dtx_dram", [D, TP], F32)
    dint("bc_dram", [32, TP], F32)
    dint("y_dram", [D, TP], F32)
    dint("xmid_dram", [D, TP], F32)
    dint("h_dram", [INTER, TP], BF16)

    P = Plan()

    with ExitStack() as ctx:
        E = ctx.enter_context
        sb = {}

        def S(stack, name, shape, dtype):
            sb[name] = stack.enter_context(nc.sbuf_tensor(name, shape, dtype))
            return sb[name]

        S(ctx, "ones_bf", [128, 1], BF16)
        S(ctx, "ones_f", [1, 128], F32)
        S(ctx, "carry_c", [128, 1], F32)
        S(ctx, "shif8_c", [128, 1], I32)
        S(ctx, "shifr_c", [128, 86], I32)
        S(ctx, "srow", [1, TP], F32)
        S(ctx, "srow2", [1, TP], F32)
        S(ctx, "srow3", [1, TP], F32)
        S(ctx, "m_rep", [128, TP], F32)
        S(ctx, "r_rep", [128, TP], F32)
        S(ctx, "hstate", [128, NDT * NST], F32)
        S(ctx, "minit", [128, NDT * NST], F32)

        S(ctx, "cvals", [1, 2], F32)
        P.emit("vector", lambda h: h.memset(sb["cvals"][0:1, 0:1], EPS))
        P.step("vector", lambda h: h.memset(sb["cvals"][0:1, 1:2], -1.0))
        nc.const_aps.register(F32, EPS, sb["cvals"][0:1, 0:1])
        nc.const_aps.register(F32, -1.0, sb["cvals"][0:1, 1:2])

        P.dma("sync", lambda h: h.dma_start(out=sb["carry_c"][:], in_=Tx["carry"]), ch="cst")
        P.dma("sync", lambda h: h.dma_start(out=sb["shif8_c"][:], in_=Tx["shif8"]), ch="cst")
        tok_c0 = P.dma("sync", lambda h: h.dma_start(out=sb["shifr_c"][:], in_=Tx["shifr"]), ch="cst")
        P.waits("vector", [tok_c0])
        P.step("vector", lambda h: h.memset(sb["ones_bf"][:], 1.0))
        P.step("vector", lambda h: h.memset(sb["ones_f"][:], 1.0))
        P.step("vector", lambda h: h.memset(sb["hstate"][:], 0.0))

        with ExitStack() as st:
            S(st, "qt0", [128, INTER // 8], I32)
            S(st, "qt1", [128, INTER // 8], I32)
            S(st, "w40", [128, INTER], I32)
            S(st, "w41", [128, INTER], I32)
            S(st, "wbf0", [128, INTER], BF16)
            S(st, "wbf1", [128, INTER], BF16)
            for b in range(2):
                S(st, f"zq{b}", [128, 86], I32)
                S(st, f"z1{b}", [128, 86], F32)
                S(st, f"zs{b}", [128, 86], F32)
                S(st, f"scg{b}", [128, 86], F32)
            phase_dequant(nc, P, sb, Tx)
            P.barrier()

        with ExitStack() as st:
            S(st, "identi", [128, 128], I32)
            S(st, "identw", [128, 128], BF16)
            S(st, "wtin0", [128, 512], BF16)
            S(st, "wtin1", [128, 512], BF16)
            S(st, "wstage4", [128, 4 * D], BF16)
            with ExitStack() as stp:
                ptr = [stp.enter_context(nc.psum_tensor(f"ptr{i}", [128, 512], BF16))
                       for i in range(8)]
                phase_transpose(nc, P, sb, Tx, ptr)
            P.barrier()

        with ExitStack() as st:
            S(st, "a_c", [128, NDT * NST], F32)
            S(st, "dvec_c", [128, NDT], F32)
            sb["bcl"] = sb["srow3"]

            with ExitStack() as sta:
                S(sta, "alog_c", [128, NDT * NST], F32)
                P.dma("sync", lambda h: h.dma_start(out=sb["alog_c"][:], in_=Tx["alog"]), ch="cst")
                tok_cl = P.dma("sync", lambda h: h.dma_start(out=sb["dvec_c"][:], in_=Tx["dvec"]), ch="cst")
                P.waits("scalar", [tok_cl])
                tok_ae = P.step("scalar", lambda h: h.activation(
                    sb["a_c"][:], sb["alog_c"][:], AF.Exp))
                P.wait("vector", tok_ae)
                P.step("vector", lambda h: h.tensor_scalar(
                    sb["a_c"][:], sb["a_c"][:], -1.0, None, ALU.mult))
                P.barrier()

            for p in (0, 1):
                xsrc = Tx["x1T"] if p == 0 else Tx["x2T"]
                with ExitStack() as st0:
                    S(st0, f"xpw_bf{p}", [128, NDT * 160], BF16)
                    S(st0, f"dtw_bf{p}", [128, D], BF16)
                    S(st0, f"dtb_c{p}", [128, NDT], F32)
                    S(st0, f"xt0_{p}", [128, TP], F32)
                    S(st0, f"xt1_{p}", [128, TP], F32)
                    S(st0, f"xb0_{p}", [128, TP], BF16)
                    S(st0, f"xsq_{p}", [128, TP], BF16)
                    S(st0, f"xln_{p}", [128, TP], F32)
                    S(st0, f"tA_{p}", [128, TP], F32)
                    S(st0, f"tB_{p}", [128, TP], F32)
                    S(st0, f"tC_{p}", [128, TP], F32)
                    S(st0, f"dtin_bf{p}", [128, TP], BF16)
                    S(st0, f"xp_acc{p}", [128, TP], F32)
                    S(st0, f"bc_acc{p}", [32, TP], F32)
                    tok_dtb = P.dma("sync", (lambda p_: lambda h: h.dma_start(
                        out=sb[f"dtb_c{p_}"][:], in_=Tx["dtb"]))(p), ch="cst")
                    tok_xpw = P.dma("gpsimd", (lambda p_: lambda h: h.dma_start(
                        out=sb[f"xpw_bf{p_}"][:].rearrange("p (n r) -> p n r", r=160),
                        in_=Tx["xpwT"].rearrange("(n p) r -> p n r", p=128)))(p), ch="gcst")
                    tok_dtw = P.dma("gpsimd", (lambda p_: lambda h: h.dma_start(
                        out=sb[f"dtw_bf{p_}"][:], in_=Tx["dtwT"]))(p), ch="gcst")
                    P.waits("tensor", [tok_xpw, tok_dtw, tok_dtb])
                    P.waits("scalar", [tok_dtb])
                    with nc.psum_tensor(f"pxa{p}", [128, TP], F32) as pxa, \
                         nc.psum_tensor(f"pxb{p}", [32, TP], F32) as pxb:
                        phase_s0(nc, P, sb, Tx, pxa, pxb, p, xsrc)
                    P.barrier()
                with ExitStack() as st1:
                    for nm in ("brep0", "brep1", "brep2", "brep3",
                               "crep0", "crep1", "crep2", "crep3",
                               "sdt0", "sdt1", "sdx0", "sdx1",
                               "da0", "da1", "uu0", "uu1", "hh0", "hh1",
                               "yacc"):
                        S(st1, f"{nm}_{p}", [128, TP], F32)
                    with nc.psum_tensor(f"prp{p}", [128, TP], F32) as prp:
                        phase_s1(nc, P, sb, Tx, prp, p)
                    P.barrier()
            with ExitStack() as st2:
                S(st2, "xt0_2", [128, TP], F32)
                S(st2, "xt1_2", [128, TP], F32)
                S(st2, "yld0", [128, TP], F32)
                S(st2, "yld1", [128, TP], F32)
                S(st2, "xln_2", [128, TP], F32)
                phase_s2(nc, P, sb, Tx)
                P.barrier()

        with nc.psum_tensor("pfa", [128, TP], F32) as pfa, \
             nc.psum_tensor("pfb", [128, TP], F32) as pfb:
            with ExitStack() as st:
                S(st, "fb0", [128, TP], BF16)
                S(st, "fsq", [128, TP], BF16)
                S(st, "xm0", [128, TP], F32)
                S(st, "xm1", [128, TP], F32)
                ffn_ln2(nc, P, sb, Tx, pfa, pfb)
                P.barrier()
            for th in range(2):
                with ExitStack() as stg:
                    S(stg, f"fln{th}", [128, TH], F32)
                    S(stg, f"fxm0_{th}", [128, TH], F32)
                    S(stg, f"fxm1_{th}", [128, TH], F32)
                    S(stg, f"xln2h{th}", [128, NDT * TH], BF16)
                    S(stg, f"wcA{th}", [128, 32 * 512], BF16)
                    S(stg, f"wcB{th}", [128, 32 * 512], BF16)
                    S(stg, f"gsil{th}", [128, 4 * TH], F32)
                    S(stg, f"hbf{th}", [128, TH], BF16)
                    ffn_gateup(nc, P, sb, Tx, pfa, pfb, th)
                    P.barrier()
                with ExitStack() as std:
                    S(std, f"wdL{th}", [128, 44 * 512], BF16)
                    S(std, f"wdH{th}", [128, 42 * 512], BF16)
                    S(std, f"hg0_{th}", [128, 11 * TH], BF16)
                    S(std, f"hg1_{th}", [128, 11 * TH], BF16)
                    S(std, f"oacc0_{th}", [128, TH], F32)
                    S(std, f"oacc1_{th}", [128, TH], F32)
                    S(std, f"oacc2_{th}", [128, TH], F32)
                    S(std, f"oacc3_{th}", [128, TH], F32)
                    S(std, f"oout{th}", [128, TH], F32)
                    S(std, f"dxm0_{th}", [128, TH], F32)
                    S(std, f"dxm1_{th}", [128, TH], F32)
                    ffn_down(nc, P, sb, Tx, pfa, pfb, th)
                    P.barrier()

        P.waits("sync", [P.cur(s) for s in list(P.val) if s.startswith("d_")])

        sems = {n: E(nc.semaphore(n.replace("d_", "D"))) for n in P.val}
        block = E(nc.Block())

        @block.sync
        def _(h):
            P.replay("sync", h, sems)

        @block.scalar
        def _(h):
            P.replay("scalar", h, sems)

        @block.vector
        def _(h):
            P.replay("vector", h, sems)

        @block.tensor
        def _(h):
            P.replay("tensor", h, sems)

        @block.gpsimd
        def _(h):
            P.replay("gpsimd", h, sems)

    return nc


def phase_dequant(nc, P, sb, Tx):
    jobs = [
        ("gqT", "gsT", "gzR", "wgt_dram", INTER, D),
        ("uqT", "usT", "uzR", "wut_dram", INTER, D),
        ("dqT", "dsT", "dzR", "wdt_dram", D, INTER),
    ]
    q_b = [Buf(P), Buf(P)]
    w_b = [Buf(P), Buf(P)]
    z_b = [Buf(P), Buf(P)]
    s_b = [Buf(P), Buf(P)]
    z1_b = [Buf(P), Buf(P)]
    w4_b = [Buf(P), Buf(P)]
    for qn, sn, zn, wn, N, K in jobs:
        KP, G = K // 8, K // GROUP
        for nt in range(N // 128):
            b = nt % 2
            qb = "qt0" if b == 0 else "qt1"
            wb = "wbf0" if b == 0 else "wbf1"
            n0 = nt * 128
            q_b[b].write("sync", (lambda qb_, qn_, n0_, KP_: lambda h: h.dma_start(
                out=sb[qb_][:, 0:KP_], in_=Tx[qn_][n0_:n0_ + 128, :]))(qb, qn, n0, KP),
                dma=True, ch=f"q{b}")
            zq, z1, zs, scg, w4 = (f"zq{b}", f"z1{b}", f"zs{b}",
                                   f"scg{b}", f"w4{b}")
            z_b[b].write("sync", (lambda zn_, n0_, G_, zq_: lambda h: h.dma_start(
                out=sb[zq_][:, 0:G_], in_=Tx[zn_][n0_:n0_ + 128, :]))(zn, n0, G, zq),
                dma=True, ch=f"zs{b}")
            s_b[b].write("sync", (lambda sn_, n0_, G_, sc_: lambda h: h.dma_start(
                out=sb[sc_][:, 0:G_], in_=Tx[sn_][n0_:n0_ + 128, :]))(sn, n0, G, scg),
                dma=True, ch=f"zs{b}")
            z_b[b].w = s_b[b].w
            z_b[b].rw("vector", (lambda G_, zq_: lambda h: h.tensor_tensor(
                sb[zq_][:, 0:G_], sb[zq_][:, 0:G_], sb["shifr_c"][:, 0:G_],
                ALU.logical_shift_right))(G, zq))
            z_b[b].rw("vector", (lambda G_, zq_: lambda h: h.tensor_scalar(
                sb[zq_][:, 0:G_], sb[zq_][:, 0:G_], 15, None, ALU.bitwise_and))(G, zq))
            tok = z1_b[b].write("vector", (lambda G_, zq_, z1_: lambda h: h.tensor_scalar(
                sb[z1_][:, 0:G_], sb[zq_][:, 0:G_], 1, None, ALU.add))(G, zq, z1))
            z_b[b].note_read(tok)
            P.wait("vector", s_b[b].w)
            tok_zs = z1_b[b].read("vector", (lambda G_, z1_, zs_, sc_: lambda h:
                h.scalar_tensor_tensor(
                    sb[zs_][:, 0:G_], sb[z1_][:, 0:G_], -1.0, sb[sc_][:, 0:G_],
                    ALU.mult, ALU.mult))(G, z1, zs, scg))
            s_b[b].note_read(tok_zs)
            P.wait("vector", q_b[b].w)
            for j in range(8):
                tok = w4_b[b].rw("vector", (lambda j_, qb_, KP_, K_, w4_: lambda h:
                    h.tensor_scalar(
                        sb[w4_][:, j_:K_:8], sb[qb_][:, 0:KP_], 4 * j_, 15,
                        ALU.logical_shift_right, ALU.bitwise_and))(j, qb, KP, K, w4))
                q_b[b].note_read(tok)
            P.waits("vector", w_b[b].readers + [w_b[b].w])
            P.waits("scalar", w_b[b].readers + [w_b[b].w])
            P.waits("scalar", [w4_b[b].w, tok_zs])
            last_v = last_s = None
            for g in range(G):
                if g % 3 == 2:
                    fn = (lambda g_, wb_, w4_, sc_, zs_: lambda h: h.activation(
                        sb[wb_][:, g_ * GROUP:(g_ + 1) * GROUP],
                        sb[w4_][:, g_ * GROUP:(g_ + 1) * GROUP],
                        AF.Identity, scale=sb[sc_][:, g_:g_ + 1],
                        bias=sb[zs_][:, g_:g_ + 1]))(g, wb, w4, scg, zs)
                    last_s = P.step("scalar", fn)
                else:
                    fn = (lambda g_, wb_, w4_, z1_, sc_: lambda h: h.tensor_scalar(
                        sb[wb_][:, g_ * GROUP:(g_ + 1) * GROUP],
                        sb[w4_][:, g_ * GROUP:(g_ + 1) * GROUP],
                        sb[z1_][:, g_:g_ + 1], sb[sc_][:, g_:g_ + 1],
                        ALU.subtract, ALU.mult))(g, wb, w4, z1, scg)
                    last_v = P.step("vector", fn)
            for t in (last_v, last_s):
                w4_b[b].note_read(t)
                z1_b[b].note_read(t)
                s_b[b].note_read(t)
            w_b[b].w = last_v
            w_b[b].readers = []
            P.wait("scalar", last_s)
            w_b[b].read("scalar", (lambda wb_, wn_, n0_, K_: lambda h: h.dma_start(
                out=Tx[wn_][n0_:n0_ + 128, :], in_=sb[wb_][:, 0:K_]))(wb, wn, n0, K),
                dma=True, ch=f"ws{b}")
            w_b[b].readers.append(last_s)


def phase_transpose(nc, P, sb, Tx, ptr):
    tok_io = P.step("gpsimd", lambda h: h.iota(
        sb["identi"][:], [[1, 128]], base=0, channel_multiplier=-1))
    P.wait("vector", tok_io)
    tok_id = P.step("vector", lambda h: h.tensor_scalar(
        sb["identw"][:], sb["identi"][:], 0, None, ALU.is_equal))
    P.wait("tensor", tok_id)

    N, K = D, INTER
    l_b = [Buf(P), Buf(P)]
    ps_b = [Buf(P), Buf(P)]
    stage = Buf(P)
    kblocks = [list(range(g, min(g + 4, K // 128))) for g in range(0, K // 128, 4)]
    for kb in kblocks:
        nk = len(kb)
        k0 = kb[0] * 128
        for nt in range(N // 128):
            b = nt % 2
            lb = "wtin0" if b == 0 else "wtin1"
            l_b[b].write("sync", (lambda lb_, nt_, k0_, nk_: lambda h: h.dma_start(
                out=sb[lb_][:, 0:nk_ * 128],
                in_=Tx["wdt_dram"][nt_ * 128:nt_ * 128 + 128, k0_:k0_ + nk_ * 128]))(
                lb, nt, k0, nk), dma=True, ch=f"tl{b}")
            P.wait("tensor", l_b[b].w)
            P.waits("tensor", ps_b[b].readers)
            tok_tr = None
            for j in range(nk):
                fn = (lambda lb_, j_, b_: lambda h: h.transpose(
                    ptr[4 * b_ + j_][:, 0:128], sb[lb_][:, j_ * 128:(j_ + 1) * 128],
                    sb["identw"][:]))(lb, j, b)
                tok_tr = P.step("tensor", fn) if j == nk - 1 else P.emit("tensor", fn)
            l_b[b].note_read(tok_tr)
            ps_b[b].note_write(tok_tr)
            P.wait("scalar", tok_tr)
            if nt < 2:
                P.wait("scalar", stage.readers[-1] if stage.readers else None)
            tok_ev = None
            for j in range(nk):
                fn = (lambda j_, nt_, b_: lambda h: h.copy(
                    sb["wstage4"][:, j_ * N + nt_ * 128:j_ * N + nt_ * 128 + 128],
                    ptr[4 * b_ + j_][:, 0:128]))(j, nt, b)
                tok_ev = P.step("scalar", fn) if j == nk - 1 else P.emit("scalar", fn)
            ps_b[b].note_read(tok_ev)
            stage.note_write(tok_ev) if nt == N // 128 - 1 else None
        stage.note_write(tok_ev)
        for j, kt in enumerate(kb):
            stage.read("sync", (lambda kt_, j_: lambda h: h.dma_start(
                out=Tx["wd_dram"][kt_ * 128:kt_ * 128 + 128, :],
                in_=sb["wstage4"][:, j_ * N:(j_ + 1) * N]))(kt, j), dma=True, ch="tst")


def _stats_sweep(nc, P, sb, Tx, pfa, pfb, src_dram, xbufs, bfn, sqn):
    xb_b = [Buf(P), Buf(P)]
    bf_b, sq_b = Buf(P), Buf(P)
    row_b = Buf(P)
    ps_b = Buf(P)
    P.step("vector", lambda h: h.memset(sb["srow"][0:1, :], 0.0))
    tok0 = P.step("vector", lambda h: h.memset(sb["srow2"][0:1, :], 0.0))
    row_b.note_write(tok0)
    for dt_i in range(NDT):
        b = dt_i % 2
        xb = xbufs[b]
        xb_b[b].write("sync", (lambda xb_, d_: lambda h: h.dma_start(
            out=sb[xb_][:], in_=src_dram[d_ * 128:d_ * 128 + 128, :]))(xb, dt_i),
            dma=True, ch=f"x{b}")
        P.wait("vector", xb_b[b].w)
        tok_cp = bf_b.write("vector", (lambda xb_: lambda h: h.tensor_copy(
            sb[bfn][:], sb[xb_][:]))(xb))
        xb_b[b].note_read(tok_cp)
        P.wait("scalar", xb_b[b].w)
        tok_sq = sq_b.write("scalar", (lambda xb_: lambda h: h.activation(
            sb[sqn][:], sb[xb_][:], AF.Square))(xb))
        xb_b[b].note_read(tok_sq)
        P.waits("tensor", [tok_cp, tok_sq] + ps_b.readers)
        for tc in range(4):
            P.emit("tensor", (lambda tc_: lambda h: h.matmul(
                pfa[0:1, tc_ * 512:(tc_ + 1) * 512], sb["ones_bf"][:, 0:1],
                sb[bfn][:, tc_ * 512:(tc_ + 1) * 512],
                start=True, stop=True))(tc))
        tok_mm = None
        for tc in range(4):
            fn = (lambda tc_: lambda h: h.matmul(
                pfb[0:1, tc_ * 512:(tc_ + 1) * 512], sb["ones_bf"][:, 0:1],
                sb[sqn][:, tc_ * 512:(tc_ + 1) * 512],
                start=True, stop=True))(tc)
            tok_mm = P.step("tensor", fn) if tc == 3 else P.emit("tensor", fn)
        bf_b.note_read(tok_mm)
        sq_b.note_read(tok_mm)
        ps_b.note_write(tok_mm)
        P.wait("vector", tok_mm)
        P.emit("vector", lambda h: h.tensor_tensor(
            sb["srow"][0:1, :], sb["srow"][0:1, :], pfa[0:1, 0:TP], ALU.add))
        tok_acc = P.step("vector", lambda h: h.tensor_tensor(
            sb["srow2"][0:1, :], sb["srow2"][0:1, :], pfb[0:1, 0:TP], ALU.add))
        ps_b.note_read(tok_acc)
        row_b.note_write(tok_acc)
    P.emit("vector", lambda h: h.tensor_scalar(
        sb["srow"][0:1, :], sb["srow"][0:1, :], 1.0 / D, None, ALU.mult))
    P.emit("vector", lambda h: h.tensor_scalar(
        sb["srow2"][0:1, :], sb["srow2"][0:1, :], 1.0 / D, None, ALU.mult))
    P.emit("vector", lambda h: h.tensor_tensor(
        sb["srow3"][0:1, :], sb["srow"][0:1, :], sb["srow"][0:1, :], ALU.mult))
    tok_var = P.step("vector", lambda h: h.tensor_tensor(
        sb["srow2"][0:1, :], sb["srow2"][0:1, :], sb["srow3"][0:1, :], ALU.subtract))
    P.wait("scalar", tok_var)
    tok_sqr = P.step("scalar", lambda h: h.activation(
        sb["srow2"][0:1, :], sb["srow2"][0:1, :], AF.Sqrt, bias=EPS))
    P.wait("vector", tok_sqr)
    tok_rc = P.step("vector", lambda h: h.reciprocal(
        sb["srow2"][0:1, :], sb["srow2"][0:1, :]))
    P.waits("tensor", [tok_rc] + ps_b.readers)
    tok_rep = None
    for tc in range(4):
        tok_rep = P.step("tensor", (lambda tc_: lambda h: h.matmul(
            pfa[:, tc_ * 512:(tc_ + 1) * 512], sb["ones_f"][0:1, :],
            sb["srow"][0:1, tc_ * 512:(tc_ + 1) * 512], start=True, stop=True))(tc))
    P.wait("vector", tok_rep)
    tok_me = P.step("vector", lambda h: h.tensor_copy(sb["m_rep"][:], pfa[:, 0:TP]))
    P.wait("tensor", tok_me)
    for tc in range(4):
        tok_rep = P.step("tensor", (lambda tc_: lambda h: h.matmul(
            pfa[:, tc_ * 512:(tc_ + 1) * 512], sb["ones_f"][0:1, :],
            sb["srow2"][0:1, tc_ * 512:(tc_ + 1) * 512], start=True, stop=True))(tc))
    P.wait("vector", tok_rep)
    P.step("vector", lambda h: h.tensor_copy(sb["r_rep"][:], pfa[:, 0:TP]))


def phase_s0(nc, P, sb, Tx, pxa, pxb, p, xsrc):
    xt0, xt1 = f"xt0_{p}", f"xt1_{p}"
    xb0, xsq, xln = f"xb0_{p}", f"xsq_{p}", f"xln_{p}"
    tA, tB, tC = f"tA_{p}", f"tB_{p}", f"tC_{p}"
    dtin_bf = f"dtin_bf{p}"
    xp_acc, bc_acc = f"xp_acc{p}", f"bc_acc{p}"
    xpw_bf, dtw_bf, dtb_c = f"xpw_bf{p}", f"dtw_bf{p}", f"dtb_c{p}"
    _stats_sweep(nc, P, sb, Tx, pxa, pxb, xsrc, (xt0, xt1), xb0, xsq)
    P.barrier()

    xb_b = [Buf(P), Buf(P)]
    ln_b, lbf_b = Buf(P), Buf(P)
    acc_b, ps_b = Buf(P), Buf(P)
    P.step("vector", lambda h: h.memset(sb[xp_acc][:], 0.0))
    tok0 = P.step("vector", lambda h: h.memset(sb[bc_acc][0:32, :], 0.0))
    acc_b.note_write(tok0)
    for dt_i in range(NDT):
        b = dt_i % 2
        xb = xt0 if b == 0 else xt1
        xb_b[b].write("sync", (lambda xb_, d_: lambda h: h.dma_start(
            out=sb[xb_][:], in_=xsrc[d_ * 128:d_ * 128 + 128, :]))(xb, dt_i),
            dma=True, ch=f"x{b}")
        P.wait("vector", xb_b[b].w)
        ln_b.write("vector", (lambda xb_: lambda h: h.tensor_tensor(
            sb[xln][:], sb[xb_][:], sb["m_rep"][:], ALU.subtract))(xb))
        tok_ln = ln_b.rw("vector", lambda h: h.tensor_tensor(
            sb[xln][:], sb[xln][:], sb["r_rep"][:], ALU.mult))
        xb_b[b].note_read(tok_ln)
        tok_bf = lbf_b.write("vector", lambda h: h.tensor_copy(sb[xb0][:], sb[xln][:]))
        ln_b.note_read(tok_bf)
        P.waits("tensor", [tok_bf] + ps_b.readers)
        for tc in range(4):
            P.emit("tensor", (lambda tc_, d_: lambda h: h.matmul(
                pxa[:, tc_ * 512:(tc_ + 1) * 512],
                sb[xpw_bf][:, d_ * 160:d_ * 160 + 128],
                sb[xb0][:, tc_ * 512:(tc_ + 1) * 512],
                start=True, stop=True))(tc, dt_i))
        tok_mm = None
        for tc in range(4):
            fn = (lambda tc_, d_: lambda h: h.matmul(
                pxb[0:32, tc_ * 512:(tc_ + 1) * 512],
                sb[xpw_bf][:, d_ * 160 + 128:d_ * 160 + 160],
                sb[xb0][:, tc_ * 512:(tc_ + 1) * 512],
                start=True, stop=True))(tc, dt_i)
            tok_mm = P.step("tensor", fn) if tc == 3 else P.emit("tensor", fn)
        lbf_b.note_read(tok_mm)
        ps_b.note_write(tok_mm)
        P.wait("vector", tok_mm)
        P.emit("vector", lambda h: h.tensor_tensor(
            sb[xp_acc][:], sb[xp_acc][:], pxa[:, 0:TP], ALU.add))
        tok_acc = P.step("vector", lambda h: h.tensor_tensor(
            sb[bc_acc][0:32, :], sb[bc_acc][0:32, :], pxb[0:32, 0:TP], ALU.add))
        ps_b.note_read(tok_acc)
        acc_b.note_write(tok_acc)
    tok_dtin = P.step("vector", (lambda: lambda h: h.tensor_copy(
        sb[dtin_bf][:], sb[xp_acc][:]))())
    P.wait("sync", tok_dtin)
    P.dma("sync", lambda h: h.dma_start(
        out=Tx["bc_dram"][:, :], in_=sb[bc_acc][0:32, :]), ch="bc")
    P.barrier()

    xb_b = [Buf(P), Buf(P)]
    tA_b, tB_b, tC_b, ln_b = Buf(P), Buf(P), Buf(P), Buf(P)
    ps_b = Buf(P)
    for dt_i in range(NDT):
        b = dt_i % 2
        xb = xt0 if b == 0 else xt1
        xb_b[b].write("sync", (lambda xb_, d_: lambda h: h.dma_start(
            out=sb[xb_][:], in_=xsrc[d_ * 128:d_ * 128 + 128, :]))(xb, dt_i),
            dma=True, ch=f"x{b}")
        tok_dmm = None
        P.waits("tensor", ps_b.readers)
        for tc in range(4):
            tok_dmm = P.step("tensor", (lambda tc_, d_: lambda h: h.matmul(
                pxa[:, tc_ * 512:(tc_ + 1) * 512],
                sb[dtw_bf][:, d_ * 128:d_ * 128 + 128],
                sb[dtin_bf][:, tc_ * 512:(tc_ + 1) * 512],
                start=True, stop=True))(tc, dt_i))
        ps_b.note_write(tok_dmm)
        P.wait("scalar", tok_dmm)
        t1 = tA_b.write("scalar", (lambda d_: lambda h: h.activation(
            sb[tA][:], pxa[:, 0:TP], AF.Relu, bias=sb[dtb_c][:, d_:d_ + 1]))(dt_i))
        t2 = tB_b.write("scalar", (lambda d_: lambda h: h.activation(
            sb[tB][:], pxa[:, 0:TP], AF.Abs, bias=sb[dtb_c][:, d_:d_ + 1]))(dt_i))
        ps_b.note_read(t1)
        ps_b.note_read(t2)
        tok3 = tC_b.write("scalar", lambda h: h.activation(
            sb[tC][:], sb[tB][:], AF.Exp, scale=-1.0))
        tB_b.note_read(tok3)
        tok4 = tB_b.write("scalar", lambda h: h.activation(
            sb[tB][:], sb[tC][:], AF.Ln, bias=1.0))
        tC_b.note_read(tok4)
        P.waits("vector", [tA_b.w, tB_b.w])
        tok_dt = tC_b.write("vector", lambda h: h.tensor_tensor(
            sb[tC][:], sb[tA][:], sb[tB][:], ALU.add))
        tA_b.note_read(tok_dt)
        tB_b.note_read(tok_dt)
        P.wait("vector", xb_b[b].w)
        ln_b.write("vector", (lambda xb_: lambda h: h.tensor_tensor(
            sb[xln][:], sb[xb_][:], sb["m_rep"][:], ALU.subtract))(xb))
        tok_ln = ln_b.rw("vector", lambda h: h.tensor_tensor(
            sb[xln][:], sb[xln][:], sb["r_rep"][:], ALU.mult))
        xb_b[b].note_read(tok_ln)
        tok_dtx = tA_b.write("vector", lambda h: h.tensor_tensor(
            sb[tA][:], sb[tC][:], sb[xln][:], ALU.mult))
        tC_b.note_read(tok_dtx)
        ln_b.note_read(tok_dtx)
        tC_b.read("sync", (lambda d_: lambda h: h.dma_start(
            out=Tx["dt_dram"][d_ * 128:d_ * 128 + 128, :], in_=sb[tC][:]))(dt_i),
            dma=True, ch="dts")
        tA_b.read("sync", (lambda d_: lambda h: h.dma_start(
            out=Tx["dtx_dram"][d_ * 128:d_ * 128 + 128, :], in_=sb[tA][:]))(dt_i),
            dma=True, ch="dxs")


def phase_s1(nc, P, sb, Tx, prp, p):
    global Tx_G
    Tx_G = Tx
    breps = [f"brep{i}_{p}" for i in range(4)]
    creps = [f"crep{i}_{p}" for i in range(4)]
    if p == 1:
        P.step("vector", lambda h: h.tensor_scalar(
            sb["minit"][:], sb["hstate"][:], sb["carry_c"][:, 0:1], None, ALU.mult))
    bcl_b = Buf(P)
    prp_b = Buf(P)
    rep_b = {nm: Buf(P) for nm in breps + creps}
    sdt_b = [Buf(P), Buf(P)]
    sdx_b = [Buf(P), Buf(P)]
    da_b = [Buf(P), Buf(P)]
    uu_b = [Buf(P), Buf(P)]
    hh_b = [Buf(P), Buf(P)]
    y_b = Buf(P)
    hstate_b = Buf(P)
    pend = []
    yacc_pend = []
    for nb in range(4):
        for q in range(4):
            n = nb * 4 + q
            rows = [(n, breps[q])] + ([(16 + n, creps[q])] if p == 1 else [])
            for row, dst in rows:
                bcl_b.write("sync", (lambda row_: lambda h: h.dma_start(
                    out=sb["bcl"][0:1, :], in_=Tx["bc_dram"][row_:row_ + 1, :]))(row),
                    dma=True, ch="bcl")
                P.wait("tensor", bcl_b.w)
                P.waits("tensor", prp_b.readers)
                tok_rp = None
                for tc in range(4):
                    tok_rp = P.step("tensor", (lambda tc_: lambda h: h.matmul(
                        prp[:, tc_ * 512:(tc_ + 1) * 512], sb["ones_f"][0:1, :],
                        sb["bcl"][0:1, tc_ * 512:(tc_ + 1) * 512],
                        start=True, stop=True))(tc))
                bcl_b.note_read(tok_rp)
                prp_b.note_write(tok_rp)
                P.wait("vector", tok_rp)
                tok_ev = rep_b[dst].write("vector", (lambda dst_: lambda h: h.tensor_copy(
                    sb[dst_][:], prp[:, 0:TP]))(dst))
                prp_b.note_read(tok_ev)
        for dt_i in range(NDT):
            b = dt_i % 2
            sdt = f"sdt0_{p}" if b == 0 else f"sdt1_{p}"
            sdx = f"sdx0_{p}" if b == 0 else f"sdx1_{p}"
            sdt_b[b].write("sync", (lambda s_, d_: lambda h: h.dma_start(
                out=sb[s_][:], in_=Tx["dt_dram"][d_ * 128:d_ * 128 + 128, :]))(sdt, dt_i),
                dma=True, ch=f"sd{b}")
            sdx_b[b].write("sync", (lambda s_, d_: lambda h: h.dma_start(
                out=sb[s_][:], in_=Tx["dtx_dram"][d_ * 128:d_ * 128 + 128, :]))(sdx, dt_i),
                dma=True, ch=f"sx{b}")
            for q in range(4):
                n = nb * 4 + q
                b2 = q % 2
                da = f"da0_{p}" if b2 == 0 else f"da1_{p}"
                uu = f"uu0_{p}" if b2 == 0 else f"uu1_{p}"
                hh = f"hh0_{p}" if b2 == 0 else f"hh1_{p}"
                P.wait("scalar", sdt_b[b].w)
                tok_da = da_b[b2].write("scalar", (lambda da_, s_, d_, n_: lambda h:
                    h.activation(sb[da_][:], sb[s_][:], AF.Exp,
                                 scale=sb["a_c"][:, d_ * NST + n_:d_ * NST + n_ + 1]))(
                    da, sdt, dt_i, n))
                sdt_b[b].note_read(tok_da)
                P.wait("gpsimd", sdx_b[b].w)
                P.wait("gpsimd", rep_b[breps[q]].w)
                tok_u = uu_b[b2].write("gpsimd", (lambda uu_, s_, q_: lambda h:
                    h.tensor_tensor(sb[uu_][:], sb[s_][:], sb[breps[q_]][:], ALU.mult))(
                    uu, sdx, q))
                sdx_b[b].note_read(tok_u)
                rep_b[breps[q]].note_read(tok_u)
                P.waits("vector", [tok_da, tok_u])
                if p == 0:
                    tok_scan = hh_b[b2].write("vector", (lambda hh_, da_, uu_: lambda h:
                        h.tensor_tensor_scan(sb[hh_][:], sb[da_][:], sb[uu_][:],
                                             0.0, ALU.mult, ALU.add))(hh, da, uu))
                    da_b[b2].note_read(tok_scan)
                    uu_b[b2].note_read(tok_scan)
                    tok_hs = hh_b[b2].read("vector", (lambda hh_, d_, n_: lambda h:
                        h.tensor_copy(sb["hstate"][:, d_ * NST + n_:d_ * NST + n_ + 1],
                                      sb[hh_][:, TP - 1:TP]))(hh, dt_i, n))
                    hstate_b.note_write(tok_hs)
                else:
                    P.wait("vector", hstate_b.w)
                    tok_scan = hh_b[b2].write("vector", (lambda hh_, da_, uu_, d_, n_:
                        lambda h: h.tensor_tensor_scan(
                            sb[hh_][:], sb[da_][:], sb[uu_][:],
                            sb["minit"][:, d_ * NST + n_:d_ * NST + n_ + 1],
                            ALU.mult, ALU.add))(hh, da, uu, dt_i, n))
                    da_b[b2].note_read(tok_scan)
                    uu_b[b2].note_read(tok_scan)
                    P.wait("gpsimd", tok_scan)
                    P.wait("gpsimd", rep_b[creps[q]].w)
                    tok_ch = uu_b[b2].write("gpsimd",
                        (lambda hh_, q_, uu_: lambda h: h.tensor_tensor(
                            sb[uu_][:], sb[hh_][:], sb[creps[q_]][:],
                            ALU.mult))(hh, q, uu))
                    hh_b[b2].note_read(tok_ch)
                    rep_b[creps[q]].note_read(tok_ch)
                    pend.append((dt_i, q, uu, tok_ch))
                    if len(pend) > 1:
                        _flush_y(P, sb, p, y_b, uu_b, pend, yacc_pend, 1)
            if p == 1:
                yacc_pend.append((dt_i, nb))
    if p == 1:
        _flush_y(P, sb, p, y_b, uu_b, pend, yacc_pend, 0)


def _flush_y(P, sb, p, y_b, ch_b2, pend, yacc_pend, keep):
    while len(pend) > keep:
        dt_i, q, cv, tok_ch = pend.pop(0)
        P.wait("vector", tok_ch)
        if q == 0:
            if yacc_pend:
                pd, pnb = yacc_pend.pop(0)
                _store_y(P, sb, p, y_b, pd, pnb)
            tok_y = y_b.write("vector", (lambda cv_: lambda h: h.tensor_copy(
                sb[f"yacc_{p}"][:], sb[cv_][:]))(cv))
        else:
            tok_y = y_b.rw("vector", (lambda cv_: lambda h: h.tensor_tensor(
                sb[f"yacc_{p}"][:], sb[f"yacc_{p}"][:], sb[cv_][:], ALU.add))(cv))
        ch_b2[(dt_i * 4 + q) % 2].note_read(tok_y)
    if keep == 0:
        while yacc_pend:
            pd, pnb = yacc_pend.pop(0)
            _store_y(P, sb, p, y_b, pd, pnb)


def _store_y(P, sb, p, y_b, dt_i, nb):
    if nb == 0:
        y_b.read("gpsimd", (lambda d_: lambda h: h.dma_start(
            out=Tx_G["y_dram"][d_ * 128:d_ * 128 + 128, :],
            in_=sb[f"yacc_{p}"][:]))(dt_i), dma=True, ch="ydma")
    else:
        y_b.read("gpsimd", (lambda d_: lambda h: h.dma_start(
            out=Tx_G["y_dram"][d_ * 128:d_ * 128 + 128, :],
            in_=sb[f"yacc_{p}"][:], accum_op=ALU.add))(dt_i), dma=True, ch="ydma")


def phase_s2(nc, P, sb, Tx):
    xln = "xln_2"
    xb_b = [Buf(P), Buf(P)]
    y2_b = [Buf(P), Buf(P)]
    ln_b = Buf(P)
    for dt_i in range(NDT):
        b = dt_i % 2
        xb = "xt0_2" if b == 0 else "xt1_2"
        yb = "yld0" if b == 0 else "yld1"
        xb_b[b].write("sync", (lambda xb_, d_: lambda h: h.dma_start(
            out=sb[xb_][:], in_=Tx["x2T"][d_ * 128:d_ * 128 + 128, :]))(xb, dt_i),
            dma=True, ch=f"x{b}")
        y2_b[b].write("sync", (lambda yb_, d_: lambda h: h.dma_start(
            out=sb[yb_][:], in_=Tx["y_dram"][d_ * 128:d_ * 128 + 128, :]))(yb, dt_i),
            dma=True, ch=f"yl{b}")
        P.wait("vector", xb_b[b].w)
        ln_b.write("vector", (lambda xb_: lambda h: h.tensor_tensor(
            sb[xln][:], sb[xb_][:], sb["m_rep"][:], ALU.subtract))(xb))
        ln_b.rw("vector", lambda h: h.tensor_tensor(
            sb[xln][:], sb[xln][:], sb["r_rep"][:], ALU.mult))
        tok_st = ln_b.rw("vector", (lambda xb_, d_: lambda h: h.scalar_tensor_tensor(
            sb[xln][:], sb[xln][:], sb["dvec_c"][:, d_:d_ + 1], sb[xb_][:],
            ALU.mult, ALU.add))(xb, dt_i))
        xb_b[b].note_read(tok_st)
        P.wait("vector", y2_b[b].w)
        tok_xm = ln_b.rw("vector", (lambda yb_: lambda h: h.tensor_tensor(
            sb[xln][:], sb[xln][:], sb[yb_][:], ALU.add))(yb))
        y2_b[b].note_read(tok_xm)
        ln_b.read("sync", (lambda d_: lambda h: h.dma_start(
            out=Tx["xmid_dram"][d_ * 128:d_ * 128 + 128, :], in_=sb[xln][:]))(dt_i),
            dma=True, ch="xms")


def ffn_ln2(nc, P, sb, Tx, pfa, pfb):
    _stats_sweep(nc, P, sb, Tx, pfa, pfb, Tx["xmid_dram"], ("xm0", "xm1"), "fb0", "fsq")


def ffn_gateup(nc, P, sb, Tx, pfa, pfb, th):
    t0 = th * TH
    xb_b = [Buf(P), Buf(P)]
    ln_b = Buf(P)
    x2_b = Buf(P)
    for dt_i in range(NDT):
        b = dt_i % 2
        xb = f"fxm0_{th}" if b == 0 else f"fxm1_{th}"
        xb_b[b].write("sync", (lambda xb_, d_: lambda h: h.dma_start(
            out=sb[xb_][:],
            in_=Tx["xmid_dram"][d_ * 128:d_ * 128 + 128, t0:t0 + TH]))(xb, dt_i),
            dma=True, ch=f"x{b}")
        P.wait("vector", xb_b[b].w)
        ln_b.write("vector", (lambda xb_: lambda h: h.tensor_tensor(
            sb[f"fln{th}"][:], sb[xb_][:], sb["m_rep"][:, t0:t0 + TH],
            ALU.subtract))(xb))
        tok = ln_b.rw("vector", (lambda d_: lambda h: h.tensor_tensor(
            sb[f"xln2h{th}"][:, d_ * TH:(d_ + 1) * TH], sb[f"fln{th}"][:],
            sb["r_rep"][:, t0:t0 + TH], ALU.mult))(d_ := dt_i))
        xb_b[b].note_read(tok)
        x2_b.note_write(tok)
    P.barrier()

    wA_b, wB_b = Buf(P), Buf(P)
    g_b = [Buf(P) for _ in range(4)]
    u_b = [Buf(P) for _ in range(4)]
    sil_b = [Buf(P) for _ in range(4)]
    h_b = Buf(P)
    pq = [pfa[:, 0:TH], pfa[:, TH:2 * TH], pfb[:, 0:TH], pfb[:, TH:2 * TH]]
    NTG = [list(range(g, min(g + 4, NNT))) for g in range(0, NNT, 4)]
    for ntg in NTG:
        nw = len(ntg) * 128
        n0 = ntg[0] * 128
        for kt in range(32):
            fn = (lambda kt_, n0_, nw_: lambda h: h.dma_start(
                out=sb[f"wcA{th}"][:, kt_ * 512:kt_ * 512 + nw_],
                in_=Tx["wgt_dram"][n0_:n0_ + nw_, kt_ * 128:kt_ * 128 + 128],
                transpose=True))(kt, n0, nw)
            if kt == 0:
                P.waits("scalar", wA_b.readers + [wA_b.w])
            tokA = P.dma("scalar", fn, ch="wca")
        wA_b.w = tokA
        wA_b.readers = []
        for qi, nt in enumerate(ntg):
            P.wait("tensor", wA_b.w)
            P.waits("tensor", g_b[qi].readers + u_b[qi].readers)
            tok_g = None
            for kt in range(32):
                for tc in range(TH // 512):
                    fn = (lambda kt_, tc_, qi_: lambda h: h.matmul(
                        pq[qi_][:, tc_ * 512:(tc_ + 1) * 512],
                        sb[f"wcA{th}"][:, kt_ * 512 + qi_ * 128:kt_ * 512 + qi_ * 128 + 128],
                        sb[f"xln2h{th}"][:, kt_ * TH + tc_ * 512:kt_ * TH + (tc_ + 1) * 512],
                        start=(kt_ == 0), stop=(kt_ == 31)))(kt, tc, qi)
                    tok_g = P.step("tensor", fn) if (kt == 31 and tc == TH // 512 - 1) \
                        else P.emit("tensor", fn)
            g_b[qi].w = tok_g
            g_b[qi].readers = []
            wA_b.note_read(tok_g)
            x2_b.note_read(tok_g)
        for kt in range(32):
            fn = (lambda kt_, n0_, nw_: lambda h: h.dma_start(
                out=sb[f"wcB{th}"][:, kt_ * 512:kt_ * 512 + nw_],
                in_=Tx["wut_dram"][n0_:n0_ + nw_, kt_ * 128:kt_ * 128 + 128],
                transpose=True))(kt, n0, nw)
            if kt == 0:
                P.waits("scalar", wB_b.readers + [wB_b.w])
            tokB = P.dma("scalar", fn, ch="wcb")
        wB_b.w = tokB
        wB_b.readers = []
        for qi in range(len(ntg)):
            P.wait("scalar", g_b[qi].w)
            P.waits("scalar", sil_b[qi].readers)
            tok_sil = None
            for tc in range(TH // 512):
                tok_sil = P.step("scalar", (lambda tc_, qi_: lambda h: h.activation(
                    sb[f"gsil{th}"][:, qi_ * TH + tc_ * 512:qi_ * TH + (tc_ + 1) * 512],
                    pq[qi_][:, tc_ * 512:(tc_ + 1) * 512], AF.Silu))(tc, qi))
            sil_b[qi].w = tok_sil
            sil_b[qi].readers = []
            g_b[qi].note_read(tok_sil)
        for qi, nt in enumerate(ntg):
            P.wait("tensor", wB_b.w)
            P.wait("tensor", sil_b[qi].w)
            P.waits("tensor", u_b[qi].readers)
            tok_u = None
            for kt in range(32):
                for tc in range(TH // 512):
                    fn = (lambda kt_, tc_, qi_: lambda h: h.matmul(
                        pq[qi_][:, tc_ * 512:(tc_ + 1) * 512],
                        sb[f"wcB{th}"][:, kt_ * 512 + qi_ * 128:kt_ * 512 + qi_ * 128 + 128],
                        sb[f"xln2h{th}"][:, kt_ * TH + tc_ * 512:kt_ * TH + (tc_ + 1) * 512],
                        start=(kt_ == 0), stop=(kt_ == 31)))(kt, tc, qi)
                    tok_u = P.step("tensor", fn) if (kt == 31 and tc == TH // 512 - 1) \
                        else P.emit("tensor", fn)
            u_b[qi].w = tok_u
            u_b[qi].readers = []
            wB_b.note_read(tok_u)
            x2_b.note_read(tok_u)
        for qi, nt in enumerate(ntg):
            P.waits("vector", h_b.readers + [h_b.w])
            tok_h = None
            for tc in range(TH // 512):
                P.waits("vector", [u_b[qi].w, sil_b[qi].w])
                tok_h = P.step("vector", (lambda tc_, qi_: lambda h: h.tensor_tensor(
                    sb[f"hbf{th}"][:, tc_ * 512:(tc_ + 1) * 512],
                    pq[qi_][:, tc_ * 512:(tc_ + 1) * 512],
                    sb[f"gsil{th}"][:, qi_ * TH + tc_ * 512:qi_ * TH + (tc_ + 1) * 512],
                    ALU.mult))(tc, qi))
                u_b[qi].note_read(tok_h)
                sil_b[qi].note_read(tok_h)
            h_b.w = tok_h
            h_b.readers = []
            h_b.read("sync", (lambda nt_: lambda h: h.dma_start(
                out=Tx["h_dram"][nt_ * 128:nt_ * 128 + 128, t0:t0 + TH],
                in_=sb[f"hbf{th}"][:, 0:TH]))(nt), dma=True, ch="hst")


def ffn_down(nc, P, sb, Tx, pfa, pfb, th):
    t0 = th * TH
    KIG = [list(range(g, min(g + 11, NNT))) for g in range(0, NNT, 11)]
    wdL_b, wdH_b = Buf(P), Buf(P)
    hg_b = [Buf(P), Buf(P)]
    ps_b = [Buf(P) for _ in range(4)]
    oac_b = [Buf(P) for _ in range(4)]
    xm_b = [Buf(P), Buf(P)]
    o_b = Buf(P)
    pview = [pfa[:, 0:TH], pfa[:, TH:2 * TH], pfb[:, 0:TH], pfb[:, TH:2 * TH]]
    for blk in range(NDT // 4):
        dds = [4 * blk + i for i in range(4)]
        for half, (wd_b, k0, k1) in enumerate(
                ((wdL_b, 0, 44), (wdH_b, 44, NNT))):
            wnm = f"wdL{th}" if half == 0 else f"wdH{th}"
            for kt in range(k0, k1):
                fn = (lambda kt_, c0_, wnm_, k0_: lambda h: h.dma_start(
                    out=sb[wnm_][:, (kt_ - k0_) * 512:(kt_ - k0_ + 1) * 512],
                    in_=Tx["wd_dram"][kt_ * 128:kt_ * 128 + 128, c0_:c0_ + 512]))(
                    kt, dds[0] * 128, wnm, k0)
                if kt == k0:
                    P.waits("scalar", wd_b.readers + [wd_b.w])
                tok_wd = P.dma("scalar", fn, ch=("wdl" if half == 0 else "wdh"))
            wd_b.w = tok_wd
            wd_b.readers = []
        for i in range(4):
            P.waits("vector", oac_b[i].readers)
            tok_z = P.step("vector", (lambda i_: lambda h: h.memset(
                sb[f"oacc{i_}_{th}"][:], 0.0))(i))
            oac_b[i].note_write(tok_z)
        for gi, kig in enumerate(KIG):
            g2 = gi % 2
            hb = f"hg{g2}_{th}"
            hg_b[g2].write("sync", (lambda hb_, k0_: lambda h: h.dma_start(
                out=sb[hb_][:, 0:TH],
                in_=Tx["h_dram"][k0_ * 128:k0_ * 128 + 128, t0:t0 + TH]))(hb, kig[0]),
                dma=True, ch=f"hg{g2}")
            tok_h = hg_b[g2].w
            for j, kt in enumerate(kig[1:], start=1):
                tok_h = P.dma("sync", (lambda hb_, j_, kt_: lambda h: h.dma_start(
                    out=sb[hb_][:, j_ * TH:(j_ + 1) * TH],
                    in_=Tx["h_dram"][kt_ * 128:kt_ * 128 + 128, t0:t0 + TH]))(hb, j, kt),
                    ch=f"hg{g2}")
            hg_b[g2].w = tok_h
            P.wait("tensor", tok_h)
            if gi == 0:
                P.wait("tensor", wdL_b.w)
            if gi == 4:
                P.wait("tensor", wdH_b.w)
            for i in range(4):
                P.waits("tensor", ps_b[i].readers)
            tok_grp = None
            nmm = 0
            for i in range(4):
                for j, kt in enumerate(kig):
                    for tc in range(TH // 512):
                        nmm += 1
                        wnm = f"wdL{th}" if kt < 44 else f"wdH{th}"
                        kr = kt if kt < 44 else kt - 44
                        fn = (lambda i_, kr_, tc_, hb_, j_, wnm_, first, last: lambda h:
                            h.matmul(
                                pview[i_][:, tc_ * 512:(tc_ + 1) * 512],
                                sb[wnm_][:, kr_ * 512 + i_ * 128:
                                          kr_ * 512 + i_ * 128 + 128],
                                sb[hb_][:, j_ * TH + tc_ * 512:j_ * TH + (tc_ + 1) * 512],
                                start=first, stop=last))(i, kr, tc, hb, j, wnm,
                                                         j == 0, j == len(kig) - 1)
                        tok_grp = P.step("tensor", fn) \
                            if (i == 3 and j == len(kig) - 1 and tc == TH // 512 - 1) \
                            else P.emit("tensor", fn)
            hg_b[g2].note_read(tok_grp)
            (wdL_b if gi <= 3 else wdH_b).note_read(tok_grp)
            for i in range(4):
                ps_b[i].note_write(tok_grp)
            P.wait("vector", tok_grp)
            for i in range(4):
                tok_e = oac_b[i].rw("vector", (lambda i_: lambda h: h.tensor_tensor(
                    sb[f"oacc{i_}_{th}"][:], sb[f"oacc{i_}_{th}"][:], pview[i_],
                    ALU.add))(i))
                ps_b[i].note_read(tok_e)
        for i, dd in enumerate(dds):
            xb = f"dxm{i % 2}_{th}"
            xm_b[i % 2].write("sync", (lambda xb_, d_: lambda h: h.dma_start(
                out=sb[xb_][:, 0:TH],
                in_=Tx["xmid_dram"][d_ * 128:d_ * 128 + 128, t0:t0 + TH]))(xb, dd),
                dma=True, ch=f"fx{i % 2}")
            P.waits("vector", [xm_b[i % 2].w, oac_b[i].w])
            tok_o = o_b.write("vector", (lambda xb_, i_: lambda h: h.tensor_tensor(
                sb[f"oout{th}"][:, 0:TH], sb[f"oacc{i_}_{th}"][:], sb[xb_][:, 0:TH],
                ALU.add))(xb, i))
            xm_b[i % 2].note_read(tok_o)
            oac_b[i].note_read(tok_o)
            o_b.read("sync", (lambda d_: lambda h: h.dma_start(
                out=Tx["outT"][d_ * 128:d_ * 128 + 128, t0:t0 + TH],
                in_=sb[f"oout{th}"][:, 0:TH]))(dd), dma=True, ch="ost")


_PROG_CACHE = {}


def kernel(L0_attn_x, L0_attn_A_log, L0_attn_D, L0_attn_x_proj_w,
           L0_attn_dt_proj_w, L0_attn_dt_proj_b,
           L0_ffn_gate_qweight, L0_ffn_gate_scales, L0_ffn_gate_qzeros,
           L0_ffn_gate_g_idx, L0_ffn_up_qweight, L0_ffn_up_scales,
           L0_ffn_up_qzeros, L0_ffn_up_g_idx, L0_ffn_down_qweight,
           L0_ffn_down_scales, L0_ffn_down_qzeros, L0_ffn_down_g_idx):
    x = np.asarray(L0_attn_x)
    assert x.shape == (4, 4096, 4096)
    gidx = np.arange(D, dtype=np.int64) // GROUP
    didx = np.arange(INTER, dtype=np.int64) // GROUP
    assert np.array_equal(np.asarray(L0_ffn_gate_g_idx, dtype=np.int64), gidx)
    assert np.array_equal(np.asarray(L0_ffn_up_g_idx, dtype=np.int64), gidx)
    assert np.array_equal(np.asarray(L0_ffn_down_g_idx, dtype=np.int64), didx)

    if "nc" not in _PROG_CACHE:
        _PROG_CACHE["nc"] = build_program()
    nc = _PROG_CACHE["nc"]

    f32 = np.float32
    xpwT = np.ascontiguousarray(np.asarray(L0_attn_x_proj_w).T).astype(f32)
    dtwT = np.ascontiguousarray(np.asarray(L0_attn_dt_proj_w).T).astype(f32)
    dtb = np.ascontiguousarray(np.asarray(L0_attn_dt_proj_b).reshape(NDT, 128).T).astype(f32)
    alog = np.ascontiguousarray(
        np.asarray(L0_attn_A_log).reshape(NDT, 128, NST).transpose(1, 0, 2).reshape(
            128, NDT * NST)).astype(f32)
    dvec = np.ascontiguousarray(np.asarray(L0_attn_D).reshape(NDT, 128).T).astype(f32)
    shif8 = (4 * (np.arange(128, dtype=np.int32) % 8)).reshape(128, 1)
    shifr = np.ascontiguousarray(np.broadcast_to(shif8, (128, 86))).astype(np.int32)

    def zrep(qz):
        return np.ascontiguousarray(np.repeat(np.asarray(qz).T, 8, axis=0)).astype(np.int32)

    common = dict(
        xpwT=xpwT, dtwT=dtwT, dtb=dtb, alog=alog, dvec=dvec, shif8=shif8, shifr=shifr,
        gqT=np.ascontiguousarray(np.asarray(L0_ffn_gate_qweight).T).astype(np.int32),
        gsT=np.ascontiguousarray(np.asarray(L0_ffn_gate_scales).T).astype(f32),
        gzR=zrep(L0_ffn_gate_qzeros),
        uqT=np.ascontiguousarray(np.asarray(L0_ffn_up_qweight).T).astype(np.int32),
        usT=np.ascontiguousarray(np.asarray(L0_ffn_up_scales).T).astype(f32),
        uzR=zrep(L0_ffn_up_qzeros),
        dqT=np.ascontiguousarray(np.asarray(L0_ffn_down_qweight).T).astype(np.int32),
        dsT=np.ascontiguousarray(np.asarray(L0_ffn_down_scales).T).astype(f32),
        dzR=zrep(L0_ffn_down_qzeros),
    )

    in_maps = []
    for r in range(8):
        beta, H = r // 2, r % 2
        xT = np.ascontiguousarray(x[beta].T).astype(f32)
        m = dict(common)
        m["x1T"] = np.ascontiguousarray(xT[:, 0:TP])
        m["x2T"] = np.ascontiguousarray(xT[:, H * TP:(H + 1) * TP])
        m["carry"] = np.full((128, 1), float(H), f32)
        in_maps.append(m)

    res = run_bass_kernel_spmd(nc, in_maps, core_ids=list(range(8)))

    out = np.empty((4, 4096, 4096), np.float32)
    for r in range(8):
        beta, H = r // 2, r % 2
        out[beta, H * TP:(H + 1) * TP, :] = res.results[r]["outT"].T
    return out
